# version 10
# baseline (speedup 1.0000x reference)
import os as _os
import numpy as np
import ml_dtypes
import concourse.bass as bass
import concourse.mybir as mybir
from concourse.bass_utils import run_bass_kernel_spmd

F32 = mybir.dt.float32
BF16 = mybir.dt.bfloat16
AF = mybir.ActivationFunctionType
OP = mybir.AluOpType
AXX = mybir.AxisListType.X

D = 1024
T = 4096
QT = 256
NSLOT = 8
NPROF = [4, 8, 12, 16, 20, 24, 28, 32]
TILES = {0: [0, 3, 4, 7, 8, 11, 12, 15],
         1: [1, 2, 5, 6, 9, 10, 13, 14]}
SCALE = float(6.0 ** -0.5)
QPAIRS = [(0, 1), (0, 2), (0, 3), (1, 2), (1, 3), (2, 3)]
KPAIRS = [(2, 3), (3, 1), (1, 2), (0, 3), (2, 0), (0, 1)]
NT_K = 32
FF = 4096
RK = 256
ONE_TENSOR_SLOTS = 5


def build_nc(stage_limit=99, slot_limit=8):
    nc = bass.Bass("TRN2", target_bir_lowering=False,
                   detect_race_conditions=False,
                   dynamic_dma_scratch_size=1024)

    d_x = nc.dram_tensor("x", [T, D], F32, kind="ExternalInput")
    d_xq = nc.dram_tensor("xq", [NSLOT * QT, D], F32, kind="ExternalInput")
    d_masks = nc.dram_tensor("masks", [NSLOT, 4, 128, QT], BF16, kind="ExternalInput")
    d_ident = nc.dram_tensor("ident", [128, 128], BF16, kind="ExternalInput")
    d_wq = nc.dram_tensor("wq", [D, 128], F32, kind="ExternalInput")
    d_wk = nc.dram_tensor("wk", [D, 128], F32, kind="ExternalInput")
    d_wv = nc.dram_tensor("wv", [D, D], F32, kind="ExternalInput")
    d_bv = nc.dram_tensor("bv", [1, D], F32, kind="ExternalInput")
    d_wout = nc.dram_tensor("wout", [D, D], F32, kind="ExternalInput")
    d_bout = nc.dram_tensor("bout", [128, 8], F32, kind="ExternalInput")
    d_u1 = nc.dram_tensor("u1", [D, RK], F32, kind="ExternalInput")
    d_v1 = nc.dram_tensor("v1", [RK, FF], F32, kind="ExternalInput")
    d_b1 = nc.dram_tensor("b1", [128, 32], F32, kind="ExternalInput")
    d_u2 = nc.dram_tensor("u2", [FF, RK], F32, kind="ExternalInput")
    d_v2 = nc.dram_tensor("v2", [RK, D], F32, kind="ExternalInput")
    d_b2 = nc.dram_tensor("b2", [128, 8], F32, kind="ExternalInput")
    d_y = nc.dram_tensor("y", [NSLOT * QT, D], F32, kind="ExternalOutput")
    DBG = bool(_os.environ.get("K_DBG"))
    d_jlk = nc.dram_tensor("jlk_s", [4, 128, T], BF16,
                           kind="ExternalOutput" if DBG else "Internal")
    d_dbg = nc.dram_tensor("dbg", [128, 40960], F32, kind="ExternalOutput") if DBG else None
    d_wouts = nc.dram_tensor("wout_s", [8, 128, 1024], BF16, kind="Internal")
    d_u1s = nc.dram_tensor("u1_s", [2, 128, 1024], BF16, kind="Internal")
    d_v1s = nc.dram_tensor("v1_s", [4, 128, 2048], BF16, kind="Internal")
    d_u2s = nc.dram_tensor("u2_s", [2, 128, 4096], BF16, kind="Internal")
    d_v2s = nc.dram_tensor("v2_s", [1, 128, 2048], BF16, kind="Internal")
    d_h = nc.dram_tensor("h_s", [32, 128, QT], BF16, kind="Internal")

    steps = {k: [] for k in ("sync", "scalar", "vector", "tensor", "gpsimd")}
    floors = {k: {} for k in steps}

    class Sem:
        __slots__ = ("name", "h", "n")

        def __init__(self, name):
            self.name, self.h, self.n = name, None, 0

    sems = []

    def mksem(name):
        s = Sem(name)
        sems.append(s)
        return s

    dL = mksem("dL")
    dO = mksem("dO")
    dS = mksem("dS")
    cV = mksem("cV")
    cA = mksem("cA")
    cT = mksem("cT")
    cG = mksem("cG")
    cSC = mksem("cSC")
    cEX = mksem("cEX")
    cMK = mksem("cMK")
    cAV = mksem("cAV")

    def em(eng, f):
        steps[eng].append(f)

    def W(eng, sem, v):
        if v <= 0:
            return
        if floors[eng].get(sem.name, 0) >= v:
            return
        floors[eng][sem.name] = v
        em(eng, lambda e, s=sem, vv=v: e.wait_ge(s.h, vv))

    def I(eng, build, incs=(), waits=()):
        for (sem, v) in waits:
            W(eng, sem, v)
        for (sem, k) in incs:
            sem.n += k

        assert len(incs) <= 1, "one inc per instruction"

        def run(e, b=build, incs=tuple(incs)):
            inst = b(e)
            if incs:
                inst.then_inc(incs[0][0].h, incs[0][1])
        em(eng, run)

    def dma(qeng, qsem, out_ap, in_ap, waits=()):
        I(qeng, lambda e, o=out_ap, i=in_ap: e.dma_start(o, i),
          incs=[(qsem, 16)], waits=waits)

    guards = []

    def sb(name, shape, dtype):
        g = nc.sbuf_tensor(name, shape, dtype)
        t = g.__enter__()
        guards.append(g)
        return t

    def psum(name, shape, dtype):
        g = nc.psum_tensor(name, shape, dtype)
        t = g.__enter__()
        guards.append(g)
        return t

    ps_s0 = psum("ps_s0", [128, 1024], F32)
    ps_s1 = psum("ps_s1", [128, 1024], F32)
    ps_nAB = psum("ps_nAB", [128, 512], F32)
    ps_nCD = psum("ps_nCD", [128, 512], F32)
    ps_den = psum("ps_den", [128, 512], F32)
    ps_u = psum("ps_u", [128, 512], F32)

    v_sb = sb("v_sb", [128, NT_K * 1024], BF16)
    lq = [sb(f"lq{G}", [128, NSLOT * QT], BF16) for G in range(4)]
    mbuf = [sb(f"mbuf{i}", [128, 4 * QT], BF16) for i in range(2)]
    p_sb = [sb(f"p{j}", [128, 1024], BF16) for j in range(8)]
    jbuf = [sb(f"jbuf{i}", [128, 4096], BF16) for i in range(2)]
    wbuf = [sb(f"wbuf{i}", [128, 4096], BF16) for i in range(2)]
    wv_sb = sb("wv_sb", [128, 8 * 1024], BF16)
    wq_sb = sb("wq_sb", [128, 8 * 128], BF16)
    wk_sb = sb("wk_sb", [128, 8 * 128], BF16)
    ao_sb = sb("ao_sb", [128, 8 * QT], BF16)
    hwb = [sb(f"hwb{i}", [128, QT], BF16) for i in range(2)]
    hrb = [sb(f"hrb{i}", [128, 2 * QT], BF16) for i in range(2)]
    xin = sb("xin", [128, D], F32)
    xnb = sb("xnb", [128, D], BF16)
    lnxt = sb("lnxt", [128, 8 * 128], BF16)
    ln2t = sb("ln2t", [128, 8 * QT], BF16)
    tmbf = [sb(f"tmbf{i}", [128, 512], BF16) for i in range(2)]
    jstage = sb("jstage", [128, 512], BF16)
    e_t1 = sb("e_t1", [128, 128], F32)
    e_t2 = sb("e_t2", [128, 128], F32)
    e_sq = sb("e_sq", [128, 96], F32)
    e_ss = sb("e_ss", [128, 16], F32)
    e_rs = sb("e_rs", [128, 16], F32)
    k12 = sb("k12", [128, 128], F32)
    x2_sb = sb("x2_sb", [128, 2 * D], F32)
    yt_sb = sb("yt_sb", [128, 2 * D], F32)
    ytp = sb("ytp", [128, 256], F32)
    p1_sb = sb("p1_sb", [128, 2 * QT], BF16)
    q2_sb = sb("q2_sb", [128, 2 * QT], BF16)
    stage = [sb(f"stage{i}", [128, 1024], F32) for i in range(2)]
    bstage = [sb(f"bstage{i}", [128, 1024], BF16) for i in range(2)]
    xqin = sb("xqin", [128, D], F32)
    bn_sb = sb("bn_sb", [128, 12], F32)
    mv_sb = sb("mv_sb", [128, 2], F32)
    rstd_sb = sb("rstd_sb", [128, 1], F32)
    eps_sb = sb("eps_sb", [128, 1], F32)
    eps24_sb = sb("eps24_sb", [128, 1], F32)
    ident_sb = sb("ident_sb", [128, 128], BF16)
    identf_sb = sb("identf_sb", [128, 128], F32)
    ones_bf = sb("ones_bf", [128, 1], BF16)
    onesr = sb("onesr", [128, 128], BF16)
    bv_sb = sb("bv_sb", [1, D], BF16)
    b1_sb = sb("b1_sb", [128, 32], F32)
    bout_sb = sb("bout_sb", [128, 8], F32)
    b2_sb = sb("b2_sb", [128, 8], F32)
    rdf_ap = sb("rdf", [128, 1024], F32).ap()
    spc_sb = sb("spc", [128, 4], F32)
    rb_sb = sb("rb_sb", [128, 512], F32)
    rd_sb = sb("rd_sb", [128, 1024], BF16)

    wv_view = wv_sb.ap().rearrange("p (c n) -> p c n", c=8)
    s1bf = ps_s1.ap().bitcast(BF16)
    ubf = ps_u.ap().bitcast(BF16)

    def mm(out, lhsT, rhs, start=True, stop=True, tp=None, incs=(), waits=()):
        I("tensor",
          lambda e, o=out, l=lhsT, r=rhs, st=start, sp=stop, tpp=tp:
          e.matmul(o, l, r, start=st, stop=sp, tile_position=tpp,
                   skip_group_check=True),
          incs=incs, waits=waits)

    def tr(out, in_, incs=(), waits=()):
        I("tensor",
          lambda e, o=out, i=in_, idn=(identf_sb if in_.dtype == F32 else ident_sb):
          e.transpose(o, i, idn[:]),
          incs=incs, waits=waits)

    def act(out, in_, func, bias=0.0, scale=1.0, incs=(), waits=()):
        I("scalar",
          lambda e, o=out, i=in_, f=func, b=bias, s=scale:
          e.activation(o, i, f, bias=b, scale=s),
          incs=incs, waits=waits)

    guard = {"xin": 0, "xnb": 0, "tmbf0": 0, "tmbf1": 0, "ao": 0}

    def vfence():
        W("vector", cV, cV.n)

    def ln_tile(src_ap, dst_ap, dst_guard_T=0):
        I("vector", lambda e, s=src_ap: e.bn_stats(bn_sb[:, 0:6], s[:, 0:512]))
        I("vector", lambda e, s=src_ap: e.bn_stats(bn_sb[:, 6:12], s[:, 512:1024]),
          incs=[(cV, 1)])
        vfence()
        I("vector", lambda e: e.bn_aggr(mv_sb[:], bn_sb[:]), incs=[(cV, 1)])
        act(rstd_sb[:], mv_sb[:, 1:2], AF.Ln, bias=eps_sb[:], waits=[(cV, cV.n)])
        act(rstd_sb[:], rstd_sb[:], AF.Exp, scale=-0.5, incs=[(cA, 1)])
        I("vector",
          lambda e, s=src_ap, d=dst_ap: e.tensor_scalar(
              d, s, mv_sb[:, 0:1], rstd_sb[:], op0=OP.subtract, op1=OP.mult),
          incs=[(cV, 1)], waits=[(cA, cA.n), (cT, dst_guard_T)])

    def exterior(src_ap, pairs, dst_ap, dst_guard_T=0):
        def v1c(c):
            return src_ap[:, c * 16:c * 16 + 16].rearrange(
                "p (g a) -> p g a", g=4).unsqueeze(3)

        def v2c(c):
            return src_ap[:, 64 + c * 16:64 + c * 16 + 16].rearrange(
                "p (g a) -> p g a", g=4).unsqueeze(3)

        t1v = e_t1.ap().rearrange("p (g a x) -> p g a x", g=4, a=4, x=8)
        t2v = e_t2.ap().rearrange("p (g a x) -> p g a x", g=4, a=4, x=8)
        vfence()
        for p, (a, b) in enumerate(pairs):
            I("vector", lambda e, o=t1v[:, :, :, p:p + 1], x=v1c(a), y=v2c(b):
              e.tensor_tensor(o, x, y, op=OP.mult))
            I("vector", lambda e, o=t2v[:, :, :, p:p + 1], x=v1c(b), y=v2c(a):
              e.tensor_tensor(o, x, y, op=OP.mult))
        Lsp = t1v[:, :, :, 0:6]
        I("vector", lambda e, L=Lsp: e.tensor_tensor(L, L, t2v[:, :, :, 0:6],
                                                     op=OP.subtract))
        sqv = e_sq.ap().rearrange("p (g a x) -> p g a x", g=4, a=4, x=6)
        I("vector", lambda e, L=Lsp, s=sqv: e.tensor_tensor(s, L, L, op=OP.mult))
        I("vector", lambda e, s=sqv: e.reduce_sum(
            e_ss.ap().rearrange("p (g a) -> p g a", g=4).unsqueeze(3),
            s, axis=AXX), incs=[(cV, 1)])
        act(e_rs[:], e_ss[:], AF.Ln, bias=eps24_sb[:], waits=[(cV, cV.n)])
        act(e_rs[:], e_rs[:], AF.Exp, scale=-0.5, incs=[(cA, 1)])
        rsb = e_rs.ap().rearrange("p (g a) -> p g a", g=4).unsqueeze(3) \
            .broadcast_to([128, 4, 4, 6])
        I("vector", lambda e, d=dst_ap, L=Lsp, r=rsb: e.tensor_tensor(
            d.rearrange("p (g a x) -> p g a x", g=4, a=4)[:, :, :, 0:6],
            L, r, op=OP.mult),
          incs=[(cV, 1)], waits=[(cA, cA.n), (cT, dst_guard_T)])

    castv = [0, 0]
    bst_free = [0, 0]

    def spill(piece_idx, in_ap, out_ap, to_sbuf=None):
        b = piece_idx % 2
        dma("scalar", dS, stage[b].ap()[:, 0:in_ap.size() // 128], in_ap,
            waits=[(cV, castv[b])])
        dv = dS.n
        w = out_ap.size() // 128 if out_ap is not None else to_sbuf.size() // 128
        if to_sbuf is not None:
            I("vector", lambda e, o=to_sbuf, w=w: e.tensor_copy(
                o, stage[b].ap()[:, 0:w]), incs=[(cV, 1)], waits=[(dS, dv)])
            castv[b] = cV.n
        else:
            I("vector", lambda e, w=w: e.tensor_copy(
                bstage[b].ap()[:, 0:w], stage[b].ap()[:, 0:w]),
              incs=[(cV, 1)], waits=[(dS, dv)])
            castv[b] = cV.n
            dma("scalar", dS, out_ap, bstage[b].ap().rearrange(
                "p (a n) -> p a n", a=out_ap.shape[-2] if out_ap.ndim == 3 else 1)
                if False else bstage[b].ap()[:, 0:w],
                waits=[(cV, cV.n)])

    def startup():
        I("vector", lambda e: e.memset(eps_sb[:], 1e-5))
        I("vector", lambda e: e.memset(eps24_sb[:], 1e-24))
        I("vector", lambda e: e.memset(ones_bf[:], 1.0))
        I("vector", lambda e: e.memset(onesr[:], 1.0))
        I("vector", lambda e: e.memset(e_t1[:], 0.0))
        I("vector", lambda e: e.memset(e_t2[:], 0.0))
        I("vector", lambda e: e.memset(tmbf[0][:], 0.0))
        I("vector", lambda e: e.memset(tmbf[1][:], 0.0), incs=[(cV, 1)])
        dma("sync", dL, ident_sb[:], d_ident.ap())
        I("vector", lambda e: e.tensor_copy(identf_sb[:], ident_sb[:]),
          incs=[(cV, 1)], waits=[(dL, dL.n)])

        dma("sync", dL, b1_sb[:], d_b1.ap())
        dma("sync", dL, bout_sb[:], d_bout.ap())
        dma("sync", dL, b2_sb[:], d_b2.ap())
        pi = 0
        spill(pi, d_wq.ap().rearrange("(r p) n -> p r n", p=128), None,
              to_sbuf=wq_sb[:]); pi += 1
        spill(pi, d_wk.ap().rearrange("(r p) n -> p r n", p=128), None,
              to_sbuf=wk_sb[:]); pi += 1
        dma("scalar", dS, stage[0][0:1, :], d_bv.ap(), waits=[(cV, castv[0])])
        I("vector", lambda e: e.tensor_copy(bv_sb[:], stage[0][0:1, :]),
          incs=[(cV, 1)], waits=[(dS, dS.n)])
        castv[0] = cV.n
        for r in range(8):
            spill(pi, d_wv.ap()[r * 128:(r + 1) * 128, :], None,
                  to_sbuf=wv_sb[:, r * 1024:(r + 1) * 1024]); pi += 1
        for j in range(8):
            spill(pi, d_wout.ap().rearrange("(r p) n -> p r n", p=128)
                  [:, :, j * 128:(j + 1) * 128], d_wouts.ap()[j]); pi += 1
        for m in range(2):
            spill(pi, d_u1.ap().rearrange("(r p) n -> p r n", p=128)
                  [:, :, m * 128:(m + 1) * 128], d_u1s.ap()[m]); pi += 1
        for q in range(4):
            for rh in range(2):
                spill(pi, d_v1.ap()[rh * 128:(rh + 1) * 128,
                                    q * 1024:(q + 1) * 1024],
                      d_v1s.ap()[q].rearrange("p (c x n) -> p c x n", c=8, x=2)
                      [:, :, rh, :]); pi += 1
        for m in range(2):
            for cg in range(4):
                spill(pi, d_u2.ap().rearrange("(c p) n -> p c n", p=128)
                      [:, cg * 8:(cg + 1) * 8, m * 128:(m + 1) * 128],
                      d_u2s.ap()[m][:, cg * 1024:(cg + 1) * 1024]); pi += 1
        for m in range(2):
            spill(pi, d_v2.ap()[m * 128:(m + 1) * 128, :],
                  d_v2s.ap()[0].rearrange("p (j x n) -> p j x n", j=8, x=2)
                  [:, :, m, :]); pi += 1

    ph1V = []
    ph1J = []

    def ph1_tile(t):
        dma("sync", dL, xin[:], d_x.ap()[t * 128:(t + 1) * 128, :],
            waits=[(cV, guard["xin"])])
        W("vector", dL, dL.n)
        ln_tile(xin.ap(), xnb.ap(), dst_guard_T=guard["xnb"])
        guard["xin"] = cV.n
        lnv = cV.n
        for r in range(8):
            tr(s1bf[:, r * 128:(r + 1) * 128], xnb[:, r * 128:(r + 1) * 128],
               waits=([(cV, lnv), (cA, cA.n)] if r == 0 else ()),
               incs=([(cT, 1)] if r == 7 else ()))
        guard["xnb"] = cT.n
        I("vector", lambda e: e.tensor_copy(lnxt[:], s1bf[:, 0:1024]),
          incs=[(cV, 1)], waits=[(cT, cT.n)])
        lnxtv = cV.n
        for hh in range(2):
            dst = ps_s1[:, hh * 512:(hh + 1) * 512]
            for r in range(8):
                mm(dst, lnxt[:, r * 128:(r + 1) * 128],
                   wv_view[:, r, hh * 512:(hh + 1) * 512],
                   start=(r == 0), stop=False,
                   waits=([(cV, lnxtv), (cA, cA.n)] if r == 0 else ()))
            mm(dst, onesr[0:1, :], bv_sb[0:1, hh * 512:(hh + 1) * 512],
               start=False, stop=True, incs=([(cT, 1)] if hh == 1 else ()))
        act(v_sb[:, t * 1024:(t + 1) * 1024], ps_s1[:], AF.Copy,
            incs=[(cA, 1)], waits=[(cT, cT.n)])
        ph1V.append(cA.n)
        for r in range(8):
            mm(ps_s1[:, 0:128], lnxt[:, r * 128:(r + 1) * 128],
               wk_sb[:, r * 128:(r + 1) * 128],
               start=(r == 0), stop=(r == 7),
               waits=([(cA, cA.n)] if r == 0 else ()),
               incs=([(cT, 1)] if r == 7 else ()))
        I("vector", lambda e: e.tensor_copy(k12[:], ps_s1[:, 0:128]),
          incs=[(cV, 1)], waits=[(cT, cT.n)])
        buf = t % 2
        exterior(k12.ap(), KPAIRS, tmbf[buf].ap(),
                 dst_guard_T=guard[f"tmbf{buf}"])
        extv = cV.n
        for G in range(4):
            tr(s1bf[:, 1024 + G * 128:1024 + (G + 1) * 128],
               tmbf[buf][:, G * 128:(G + 1) * 128],
               waits=([(cV, extv), (cA, cA.n)] if G == 0 else ()),
               incs=([(cT, 1)] if G == 3 else ()))
        guard[f"tmbf{buf}"] = cT.n
        I("vector", lambda e: e.tensor_copy(jstage[:], s1bf[:, 1024:1536]),
          incs=[(cV, 1)], waits=[(cT, cT.n), (dO, (ph1J[-1] if ph1J else 0))])
        dma("sync", dO,
            d_jlk.ap()[:, :, t * 128:(t + 1) * 128].transpose([1, 0, 2]),
            jstage.ap().rearrange("p (g n) -> p g n", g=4),
            waits=[(cV, cV.n)])
        ph1J.append(dO.n)

    ph1Q = [0] * NSLOT

    def ph1b_slot(si):
        for tt in range(2):
            tq = si * 2 + tt
            dma("sync", dL, xin[:], d_xq.ap()[tq * 128:(tq + 1) * 128, :],
                waits=[(cV, guard["xin"])])
            W("vector", dL, dL.n)
            ln_tile(xin.ap(), xnb.ap(), dst_guard_T=guard["xnb"])
            guard["xin"] = cV.n
            lnv = cV.n
            for r in range(8):
                tr(s1bf[:, r * 128:(r + 1) * 128], xnb[:, r * 128:(r + 1) * 128],
                   waits=([(cV, lnv), (cA, cA.n)] if r == 0 else ()),
                   incs=([(cT, 1)] if r == 7 else ()))
            guard["xnb"] = cT.n
            I("vector", lambda e: e.tensor_copy(lnxt[:], s1bf[:, 0:1024]),
              incs=[(cV, 1)], waits=[(cT, cT.n)])
            lnxv = cV.n
            for r in range(8):
                mm(ps_s1[:, 0:128], lnxt[:, r * 128:(r + 1) * 128],
                   wq_sb[:, r * 128:(r + 1) * 128],
                   start=(r == 0), stop=(r == 7),
                   waits=([(cV, lnxv), (cA, cA.n)] if r == 0 else ()),
                   incs=([(cT, 1)] if r == 7 else ()))
            I("vector", lambda e: e.tensor_copy(k12[:], ps_s1[:, 0:128]),
              incs=[(cV, 1)], waits=[(cT, cT.n)])
            buf = tt
            exterior(k12.ap(), QPAIRS, tmbf[buf].ap(),
                     dst_guard_T=guard[f"tmbf{buf}"])
            extv = cV.n
            for G in range(4):
                tr(s1bf[:, 1024 + G * 128:1024 + (G + 1) * 128],
                   tmbf[buf][:, G * 128:(G + 1) * 128],
                   waits=([(cV, extv), (cA, cA.n)] if G == 0 else ()),
                   incs=([(cT, 1)] if G == 3 else ()))
            guard[f"tmbf{buf}"] = cT.n
            for G in range(4):
                I("vector", lambda e, G=G, si=si, tt=tt: e.tensor_copy(
                    lq[G][:, si * QT + tt * 128: si * QT + (tt + 1) * 128],
                    s1bf[:, 1024 + G * 128:1024 + (G + 1) * 128]),
                  incs=([(cV, 1)] if G == 3 else ()), waits=[(cT, cT.n)])
        ph1Q[si] = cV.n

    ggi = [0]
    quad_no = [0]
    s_free = {0: 0, 1: 0}
    jbuf_free = [0, 0]
    jd = {}
    exp_after = {}
    mask_after = {}

    mload = [0] * NSLOT

    def mask_load(si):
        dma("sync", dL, mbuf[si % 2].ap().rearrange("p (j q) -> p j q", j=4),
            d_masks.ap()[si].transpose([1, 0, 2]),
            waits=[(cMK, mbuf_free[si % 2])])
        mload[si] = dL.n

    mbuf_free = [0, 0]

    def jlk_prefetch(si, G):
        n = NPROF[si]
        b = G % 2
        dma("sync", dL, jbuf[b][:, 0:n * 128], d_jlk.ap()[G][:, 0:n * 128],
            waits=[(cSC, jbuf_free[b]), (dO, ph1J[n - 1])])
        jd[(si, G)] = dL.n

    def emit_av(si, G, my_quad, gj, h_ggi, chunks, masked, ngrp):
        slabs = [p_sb[(h_ggi % 2) * 4 + a] for a in range(4)]
        first = gj == 0
        last = gj == ngrp - 1
        for ci, c in enumerate(chunks):
            w = []
            if ci == 0:
                for a in range(4):
                    if masked:
                        w.append((cMK, mask_after[(h_ggi, a)]))
                    else:
                        w.append((cEX, exp_after[(h_ggi, a)]))
                w.append((cA, ph1V[max(chunks)]))
                if first and my_quad >= 1:
                    w.append((cV, ev_vals[my_quad - 1]))
            st = first and ci == 0
            sp = last and ci == 3
            for a in range(4):
                nb = ps_nAB if a < 2 else ps_nCD
                vcol = c * 1024 + (4 * G + a) * 64
                base = 64 * (a % 2)
                mm(nb[base:base + 64, 0:256],
                   v_sb[:, vcol:vcol + 64],
                   slabs[a][:, ci * 256:(ci + 1) * 256],
                   start=st, stop=sp, tp=(0, base),
                   waits=(w if a == 0 else ()))
            dendst = [ps_nAB[0:1, 256:512], ps_den[0:1, 0:256],
                      ps_nCD[0:1, 256:512], ps_den[0:1, 256:512]]
            for a in range(4):
                mm(dendst[a], ones_bf[:],
                   slabs[a][:, ci * 256:(ci + 1) * 256],
                   start=(st if a == 1 else False), stop=sp, tp=(0, 0),
                   incs=([(cAV, 1)] if (a == 3 and ci == 3) else ()))

    def attn_quad(si, G):
        n = NPROF[si]
        ngrp = n // 4
        two_t = si >= ONE_TENSOR_SLOTS
        grp = [list(range(n - 4, n))] + \
              [list(range(4 * g, 4 * g + 4)) for g in range(ngrp - 1)]
        qv = ph1Q[si]
        my_quad = quad_no[0]
        quad_no[0] += 1
        jdv = jd[(si, G)]
        pend = []
        for gi, chunks in enumerate(grp):
            masked = gi == 0
            g_ggi = ggi[0]
            ggi[0] += 1
            slabs = [p_sb[(g_ggi % 2) * 4 + a] for a in range(4)]
            ring_wait = max(0, g_ggi - 1)
            sc_done = {}

            def sc_head(a, st):
                stv = ps_s0 if st == 0 else ps_s1
                for ci, c in enumerate(chunks):
                    mm(stv[:, ci * 256:(ci + 1) * 256],
                       jbuf[G % 2][32 * a:32 * a + 6, c * 128:(c + 1) * 128],
                       lq[G][32 * a:32 * a + 6, si * QT:(si + 1) * QT],
                       tp=(32 * a, 0),
                       waits=([(dL, jdv), (cV, qv), (cEX, s_free[st])]
                              if ci == 0 else ()),
                       incs=([(cSC, 1)] if ci == 3 else ()))
                sc_done[a] = cSC.n

            def exp_head(a, st):
                stv = ps_s0 if st == 0 else ps_s1
                act(slabs[a][:], stv[:], AF.Exp, scale=SCALE,
                    waits=[(cSC, sc_done[a]), (cAV, ring_wait)],
                    incs=[(cEX, 1)])
                exp_after[(g_ggi, a)] = cEX.n
                s_free[st] = cEX.n

            if two_t:
                sc_head(0, 0); sc_head(2, 1)
                exp_head(0, 0); exp_head(2, 1)
                sc_head(1, 0); sc_head(3, 1)
                exp_head(1, 0); exp_head(3, 1)
            else:
                for a in range(4):
                    sc_head(a, 0)
                    exp_head(a, 0)
            if masked:
                for a in range(4):
                    I("vector", lambda e, sl=slabs[a], si=si: e.tensor_tensor(
                        sl[:], sl[:], mbuf[si % 2][:], op=OP.mult),
                      waits=[(cEX, exp_after[(g_ggi, a)]), (dL, mload[si])],
                      incs=[(cMK, 1)])
                    mask_after[(g_ggi, a)] = cMK.n
            pend.append((gi, g_ggi, chunks, masked))
            if gi >= 1:
                gj, h, chs, msk = pend.pop(0)
                emit_av(si, G, my_quad, gj, h, chs, msk, ngrp)
        for (gj, h, chs, msk) in pend:
            emit_av(si, G, my_quad, gj, h, chs, msk, ngrp)

    def attn_tail(si, G):
        avv = ggi[0]
        densrc = [ps_nAB[0:1, 256:512], ps_den[0:1, 0:256],
                  ps_nCD[0:1, 256:512], ps_den[0:1, 256:512]]
        for a in range(4):
            I("vector", lambda e, a=a, s=densrc[a]: e.reciprocal(
                rdf_ap[0:1, 256 * a:256 * (a + 1)], s),
              waits=([(cAV, avv)] if a == 0 else ()),
              incs=([(cV, 1)] if a == 3 else ()))
        vfence()
        I("vector", lambda e: e.tensor_copy(rd_sb[0:1, :], rdf_ap[0:1, :]),
          incs=[(cV, 1)])
        rdv = cV.n
        bc = [(0, ps_u[0:32, 0:256], (0, 0)), (0, ps_u[32:64, 0:256], (0, 32)),
              (1, ps_u[64:96, 0:256], (0, 64)), (1, ps_u[96:128, 0:256], (0, 96)),
              (2, ps_u[0:32, 256:512], (0, 0)), (2, ps_u[32:64, 256:512], (0, 32)),
              (3, ps_u[64:96, 256:512], (0, 64)), (3, ps_u[96:128, 256:512], (0, 96))]
        for i, (a, dst, tpp) in enumerate(bc):
            mm(dst, onesr[0:1, 0:32], rd_sb[0:1, 256 * a:256 * (a + 1)],
               tp=tpp,
               waits=([(cV, max(rdv, b7V[0])), (cA, b7A[0])] if i == 0 else ()),
               incs=([(cT, 1)] if i == 7 else ()))
        bcv = cT.n
        I("vector", lambda e: e.tensor_copy(rb_sb[:], ps_u[:]),
          waits=[(cT, bcv)], incs=[(cV, 1)])
        b7V[0] = cV.n
        for a in range(4):
            h = 4 * G + a
            nsrc = ps_nAB if a < 2 else ps_nCD
            base = 64 * (a % 2)
            ucol = 0 if a < 2 else 256
            I("vector", lambda e, h=h, nsrc=nsrc, base=base, ucol=ucol:
              e.tensor_tensor(
                  ao_sb[64 * (h % 2):64 * (h % 2) + 64,
                        (h // 2) * QT:(h // 2 + 1) * QT],
                  nsrc[base:base + 64, 0:256],
                  rb_sb[base:base + 64, ucol:ucol + 256], op=OP.mult),
              waits=[(cT, guard["ao"])],
              incs=([(cV, 1)] if a == 3 else ()))
        ev_vals.append(cV.n)

    wbuf_free = [0, 0]
    ev_vals = []
    b7A = [0]
    b7V = [0]
    wpc = [0]
    hr_free = [0, 0]
    hw_free = [0, 0]
    h_guard = [0]
    out_dma = [0]
    spill_done = [0]

    def wpiece(dram_ap, ncols):
        b = wpc[0] % 2
        wpc[0] += 1
        dma("scalar", dS, wbuf[b][:, 0:ncols], dram_ap,
            waits=[(cT, wbuf_free[b]), (dS, spill_done[0])])
        return b, dS.n

    def piece_P0(si):
        for j in range(8):
            b, dv = wpiece(d_wouts.ap()[j], 1024)
            for rr in range(8):
                mm(ps_u[:, 0:256], wbuf[b][:, rr * 128:(rr + 1) * 128],
                   ao_sb[:, rr * QT:(rr + 1) * QT],
                   start=(rr == 0), stop=(rr == 7),
                   waits=([(dS, dv), (cV, ev_vals[4 * (si + 1) - 1]),
                           (cA, b7A[0]), (cV, b7V[0])]
                          if rr == 0 else ()),
                   incs=([(cT, 1)] if rr == 7 else ()))
            wbuf_free[b] = cT.n
            act(ytp[:], ps_u[:, 0:256], AF.Identity, bias=bout_sb[:, j:j + 1],
                incs=[(cA, 1)], waits=[(cT, cT.n)])
            b7A[0] = cA.n
            for tt in range(2):
                tr(ps_u[:, tt * 128:(tt + 1) * 128], ytp[:, tt * 128:(tt + 1) * 128],
                   waits=([(cA, cA.n)] if tt == 0 else ()),
                   incs=([(cT, 1)] if tt == 1 else ()))
            I("vector", lambda e, j=j: e.tensor_copy(
                yt_sb.ap().rearrange("p (t n) -> p t n", t=2)
                [:, :, j * 128:(j + 1) * 128],
                ps_u.ap()[:, 0:256].rearrange("p (t n) -> p t n", t=2)),
              incs=[(cV, 1)], waits=[(cT, cT.n)])
            b7V[0] = cV.n

    def piece_P1(si):
        for tt in range(2):
            dma("sync", dL, xqin[:], d_xq.ap()[(si * 2 + tt) * 128:
                                               (si * 2 + tt + 1) * 128, :],
                waits=[(cG, cG.n)])
            ldv = dL.n
            yv = yt_sb.ap()[:, tt * 1024:(tt + 1) * 1024]
            ln_tile(yv, yv)
            lnv = cV.n
            I("gpsimd", lambda e, tt=tt, yv=yv: e.tensor_add(
                x2_sb[:, tt * 1024:(tt + 1) * 1024], xqin[:], yv),
              waits=[(cV, lnv), (dL, ldv), (dO, out_dma[0])], incs=[(cG, 1)])
            W("vector", cG, cG.n)
            ln_tile(x2_sb.ap()[:, tt * 1024:(tt + 1) * 1024], xnb.ap(),
                    dst_guard_T=guard["xnb"])
            lnv2 = cV.n
            for half in range(2):
                for r in range(4):
                    tr(ubf[:, half * 512 + r * 128: half * 512 + (r + 1) * 128],
                       xnb[:, half * 512 + r * 128: half * 512 + (r + 1) * 128],
                       waits=([(cV, max(lnv2, b7V[0])), (cA, b7A[0])]
                              if r == 0 else ()),
                       incs=([(cT, 1)] if r == 3 else ()))
                if half == 1:
                    guard["xnb"] = cT.n
                I("vector", lambda e, tt=tt, half=half: e.tensor_copy(
                    ln2t.ap().rearrange("p (r q) -> p r q", r=8)
                    [:, half * 4:(half + 1) * 4, tt * 128:(tt + 1) * 128],
                    ubf.rearrange("p (r q) -> p r q", r=8)
                    [:, half * 4:(half + 1) * 4, :]),
                  incs=[(cV, 1)], waits=[(cT, cT.n)])
                b7V[0] = cV.n

    def piece_P2(si):
        l2v = cV.n
        for m in range(2):
            b, dv = wpiece(d_u1s.ap()[m], 1024)
            for rr in range(8):
                mm(ps_u[:, 0:256], wbuf[b][:, rr * 128:(rr + 1) * 128],
                   ln2t[:, rr * QT:(rr + 1) * QT],
                   start=(rr == 0), stop=(rr == 7),
                   waits=([(dS, dv), (cV, max(l2v, b7V[0])), (cA, b7A[0])]
                          if rr == 0 else ()),
                   incs=([(cT, 1)] if rr == 7 else ()))
            wbuf_free[b] = cT.n
            act(p1_sb[:, m * QT:(m + 1) * QT], ps_u[:, 0:256], AF.Copy,
                incs=[(cA, 1)], waits=[(cT, cT.n)])
            b7A[0] = cA.n
        p1v = cA.n
        for q in range(4):
            b, dv = wpiece(d_v1s.ap()[q], 2048)
            for cc8 in range(8):
                cc = q * 8 + cc8
                hb = cc % 2
                for rh in range(2):
                    mm(ps_u[:, 0:256],
                       wbuf[b][:, cc8 * 256 + rh * 128: cc8 * 256 + (rh + 1) * 128],
                       p1_sb[:, rh * QT:(rh + 1) * QT],
                       start=(rh == 0), stop=(rh == 1),
                       waits=([(dS, dv), (cA, max(p1v, b7A[0])), (cV, b7V[0])]
                              if rh == 0 else ()),
                       incs=([(cT, 1)] if rh == 1 else ()))
                act(hwb[hb][:], ps_u[:, 0:256], AF.Gelu_apprx_tanh,
                    bias=b1_sb[:, cc:cc + 1],
                    incs=[(cA, 1)],
                    waits=[(cT, cT.n), (dO, hw_free[hb])])
                b7A[0] = cA.n
                dma("sync", dO, d_h.ap()[cc], hwb[hb][:],
                    waits=[(cA, cA.n), (dS, h_guard[0])])
                hw_free[hb] = dO.n
            wbuf_free[b] = cT.n

    def piece_P3(si):
        hspill = dO.n
        for m in range(2):
            b, dv = wpiece(d_u2s.ap()[m], 4096)
            for ccg in range(16):
                hb = ccg % 2
                dma("scalar", dS, hrb[hb].ap().rearrange("p (c q) -> p c q", c=2),
                    d_h.ap().transpose([1, 0, 2])[:, ccg * 2:(ccg + 1) * 2, :],
                    waits=[(dO, hspill), (cT, hr_free[hb])])
                hdv = dS.n
                for c4 in range(2):
                    cc = ccg * 2 + c4
                    w = []
                    if c4 == 0:
                        w = [(dS, hdv)]
                        if cc == 0:
                            w += [(cA, b7A[0]), (cV, b7V[0])]
                    mm(ps_u[:, 0:256], wbuf[b][:, cc * 128:(cc + 1) * 128],
                       hrb[hb][:, c4 * QT:(c4 + 1) * QT],
                       start=(cc == 0), stop=(cc == 31),
                       waits=w, incs=([(cT, 1)] if c4 == 1 else ()))
                hr_free[hb] = cT.n
            wbuf_free[b] = cT.n
            act(q2_sb[:, m * QT:(m + 1) * QT], ps_u[:, 0:256], AF.Copy,
                incs=[(cA, 1)], waits=[(cT, cT.n)])
            b7A[0] = cA.n
        h_guard[0] = dS.n
        q2v = cA.n
        b, dv = wpiece(d_v2s.ap()[0], 2048)
        for j in range(8):
            for m in range(2):
                mm(ps_u[:, 0:256],
                   wbuf[b][:, j * 256 + m * 128: j * 256 + (m + 1) * 128],
                   q2_sb[:, m * QT:(m + 1) * QT],
                   start=(m == 0), stop=(m == 1),
                   waits=([(dS, dv), (cA, max(q2v, b7A[0])), (cV, b7V[0])]
                          if m == 0 else ()),
                   incs=([(cT, 1)] if m == 1 else ()))
            act(ytp[:], ps_u[:, 0:256], AF.Identity, bias=b2_sb[:, j:j + 1],
                incs=[(cA, 1)], waits=[(cT, cT.n)])
            b7A[0] = cA.n
            for tt in range(2):
                tr(ps_u[:, tt * 128:(tt + 1) * 128],
                   ytp[:, tt * 128:(tt + 1) * 128],
                   waits=([(cA, cA.n)] if tt == 0 else ()),
                   incs=([(cT, 1)] if tt == 1 else ()))
            I("vector", lambda e, j=j: e.tensor_copy(
                yt_sb.ap().rearrange("p (t n) -> p t n", t=2)
                [:, :, j * 128:(j + 1) * 128],
                ps_u.ap()[:, 0:256].rearrange("p (t n) -> p t n", t=2)),
              incs=[(cV, 1)], waits=[(cT, cT.n)])
            b7V[0] = cV.n
        wbuf_free[b] = cT.n
        for tt in range(2):
            yv = yt_sb.ap()[:, tt * 1024:(tt + 1) * 1024]
            ln_tile(yv, yv)
            lnv = cV.n
            I("gpsimd", lambda e, tt=tt, yv=yv: e.tensor_add(
                x2_sb[:, tt * 1024:(tt + 1) * 1024],
                x2_sb[:, tt * 1024:(tt + 1) * 1024], yv),
              waits=[(cV, lnv)], incs=[(cG, 1)])
            dma("sync", dO,
                d_y.ap()[(si * 2 + tt) * 128:(si * 2 + tt + 1) * 128, :],
                x2_sb[:, tt * 1024:(tt + 1) * 1024], waits=[(cG, cG.n)])
        out_dma[0] = dO.n

    pieces = [piece_P0, piece_P1, piece_P2, piece_P3]

    startup()
    spill_done[0] = dS.n
    W("tensor", dL, dL.n)
    if stage_limit < 1:
        return _realize(nc, sems, steps, dO)
    W("tensor", cV, cV.n)
    W("scalar", dL, dL.n)
    W("vector", dL, dL.n)
    W("gpsimd", dL, dL.n)
    ph1b_slot(0)
    ph1b_slot(1)
    kt = [0]
    if stage_limit < 2:
        return _realize(nc, sems, steps, dO)

    def emit_ph1_until(ntiles):
        while kt[0] < ntiles:
            ph1_tile(kt[0])
            kt[0] += 1

    if stage_limit < 3:
        for tq in range(2, 8):
            ph1b_slot(tq)
        emit_ph1_until(NT_K)
        if DBG:
            W("sync", cV, cV.n)
            W("sync", cA, cA.n)
            col = [0]

            def dump(ap, w):
                dma("sync", dO, d_dbg.ap()[:, col[0]:col[0] + w], ap)
                col[0] += w
            dump(v_sb.ap().bitcast(F32), 16384)
            dump(ao_sb.ap().bitcast(F32), 1024)
            for G in range(4):
                dump(lq[G].ap().bitcast(F32), 1024)
            W("sync", dO, dO.n)
            dma("sync", dL, jbuf[0][:], d_jlk.ap()[0])
            W("sync", dL, dL.n)
            dump(jbuf[0].ap().bitcast(F32), 2048)
        return _realize(nc, sems, steps, dO)
    ph1b_sched = {0: [2, 3], 1: [4, 5], 2: [6, 7]}
    for si in range(min(NSLOT, slot_limit)):
        emit_ph1_until(NPROF[si])
        mask_load(si)
        if si >= 2:
            mbuf_free[si % 2] = cMK.n
        if si == ONE_TENSOR_SLOTS:
            emit_ph1_until(NT_K)
            W("tensor", cA, ph1V[NT_K - 1])
            W("tensor", cV, cV.n)
        jlk_prefetch(si, 0)
        for G in range(4):
            if G < 3:
                pass
            attn_quad(si, G)
            jbuf_free[G % 2] = cSC.n
            if G < 3:
                jlk_prefetch(si, G + 1)
            if si > 0:
                pieces[G](si - 1)
                if G == 0:
                    guard["ao"] = cT.n
                if DBG and si == 1 and G == 0:
                    dma("sync", dO, d_dbg.ap()[:, 38912:39936],
                        ao_sb.ap().bitcast(F32), waits=[(cV, ev_vals[3])])
                    W("vector", dO, dO.n)
            elif si in ph1b_sched and G in (1, 3):
                ph1b_slot(ph1b_sched[si][0 if G == 1 else 1])
            attn_tail(si, G)
        if si in (1, 2) :
            for tq in ph1b_sched[si]:
                ph1b_slot(tq)
    if slot_limit >= NSLOT:
        for G in range(4):
            pieces[G](NSLOT - 1)
            if G == 0:
                guard["ao"] = cT.n
    if DBG:
        W("sync", cV, cV.n)
        W("sync", cA, cA.n)
        W("sync", cG, cG.n)
        col = [0]

        def dump(ap, w):
            dma("sync", dO, d_dbg.ap()[:, col[0]:col[0] + w], ap)
            col[0] += w
        dump(v_sb.ap().bitcast(F32), 16384)
        dump(ao_sb.ap().bitcast(F32), 1024)
        for G in range(4):
            dump(lq[G].ap().bitcast(F32), 1024)
        dump(p1_sb.ap().bitcast(F32), 256)
        dump(q2_sb.ap().bitcast(F32), 256)
        dump(ln2t.ap().bitcast(F32), 1024)
        dump(x2_sb.ap(), 2048)
        dump(yt_sb.ap(), 2048)
        dump(rb_sb.ap(), 512)
    return _realize(nc, sems, steps, dO)


def _realize(nc, sems, steps, dO):
    semguards = []
    for s in sems:
        g = nc.semaphore(s.name)
        s.h = g.__enter__()
        semguards.append(g)
    final_out = dO.n
    steps["sync"].append(lambda e: e.wait_ge(dO.h, final_out))
    blk = nc.Block()
    block = blk.__enter__()
    for eng_name, meth in (("sync", block.sync), ("scalar", block.scalar),
                           ("vector", block.vector), ("tensor", block.tensor),
                           ("gpsimd", block.gpsimd)):
        def body(e, lst=steps[eng_name]):
            for f in lst:
                f(e)
        meth(body)
    blk.__exit__(None, None, None)
    for g in semguards[::-1]:
        g.__exit__(None, None, None)
    return nc


_NC_CACHE = None


def kernel(L0_attn_x, L0_attn_W1_q, L0_attn_W2_q, L0_attn_W1_k, L0_attn_W2_k,
           L0_attn_W_v, L0_attn_b_v, L0_attn_W_out, L0_attn_b_out,
           L0_ffn_U1, L0_ffn_V1, L0_ffn_b1, L0_ffn_U2, L0_ffn_V2, L0_ffn_b2):
    global _NC_CACHE
    bf = ml_dtypes.bfloat16
    x = np.ascontiguousarray(np.asarray(L0_attn_x, dtype=np.float32))
    B = x.shape[0]

    perm = np.array([(j % 16) * 4 + j // 16 for j in range(64)])
    wq = np.ascontiguousarray(
        np.concatenate([np.asarray(L0_attn_W1_q)[perm],
                        np.asarray(L0_attn_W2_q)[perm]], axis=0).T
        .astype(np.float32))
    wk = np.ascontiguousarray(
        np.concatenate([np.asarray(L0_attn_W1_k)[perm],
                        np.asarray(L0_attn_W2_k)[perm]], axis=0).T
        .astype(np.float32))
    wv = np.ascontiguousarray(np.asarray(L0_attn_W_v, dtype=np.float32).T)
    wout = np.ascontiguousarray(np.asarray(L0_attn_W_out, dtype=np.float32).T)
    bv = np.ascontiguousarray(np.asarray(L0_attn_b_v, dtype=np.float32)[None, :])
    bout = np.ascontiguousarray(
        np.asarray(L0_attn_b_out, dtype=np.float32).reshape(8, 128).T)
    u1 = np.ascontiguousarray(np.asarray(L0_ffn_U1, dtype=np.float32))
    v1 = np.ascontiguousarray(np.asarray(L0_ffn_V1, dtype=np.float32))
    b1 = np.ascontiguousarray(
        np.asarray(L0_ffn_b1, dtype=np.float32).reshape(32, 128).T)
    u2 = np.ascontiguousarray(np.asarray(L0_ffn_U2, dtype=np.float32))
    v2 = np.ascontiguousarray(np.asarray(L0_ffn_V2, dtype=np.float32))
    b2 = np.ascontiguousarray(
        np.asarray(L0_ffn_b2, dtype=np.float32).reshape(8, 128).T)
    ident = np.eye(128, dtype=bf)

    L = np.triu(np.ones((128, 128), dtype=np.float32))
    onesm = np.ones((128, QT), dtype=np.float32)
    zerosm = np.zeros((128, QT), dtype=np.float32)
    tri_a = np.concatenate([L, np.ones((128, 128), dtype=np.float32)], axis=1)
    tri_b = np.concatenate([np.zeros((128, 128), dtype=np.float32), L], axis=1)

    def core_masks(par):
        mk = np.zeros((NSLOT, 4, 128, QT), dtype=np.float32)
        for si in range(NSLOT):
            n = NPROF[si]
            t = TILES[par][si]
            ct = 2 * t + 2
            for j in range(4):
                c = n - 4 + j
                if c < ct - 2:
                    mk[si, j] = onesm
                elif c == ct - 2:
                    mk[si, j] = tri_a
                elif c == ct - 1:
                    mk[si, j] = tri_b
                else:
                    mk[si, j] = zerosm
        return mk.astype(bf)

    masks = {p: core_masks(p) for p in (0, 1)}

    if _NC_CACHE is None:
        _NC_CACHE = build_nc()
    nc = _NC_CACHE

    in_maps = []
    for core in range(8):
        s, par = core // 2, core % 2
        xs = np.ascontiguousarray(x[s])
        xq = np.ascontiguousarray(
            np.concatenate([xs[QT * t:QT * (t + 1)] for t in TILES[par]],
                           axis=0))
        in_maps.append({
            "x": xs, "xq": xq, "masks": masks[par], "ident": ident,
            "wq": wq, "wk": wk, "wv": wv, "bv": bv, "wout": wout,
            "bout": bout, "u1": u1, "v1": v1, "b1": b1, "u2": u2, "v2": v2,
            "b2": b2,
        })

    res = run_bass_kernel_spmd(nc, in_maps, core_ids=list(range(8)))
    y = np.empty((B, T, D), dtype=np.float32)
    for core in range(8):
        s, par = core // 2, core % 2
        yc = res.results[core]["y"]
        for si, t in enumerate(TILES[par]):
            y[s, QT * t:QT * (t + 1)] = yc[QT * si:QT * (si + 1)]
    return y


# version 11
# speedup vs baseline: 1.0871x; 1.0871x over previous
import os as _os
import numpy as np
import ml_dtypes
import concourse.bass as bass
import concourse.mybir as mybir
from concourse.bass_utils import run_bass_kernel_spmd

F32 = mybir.dt.float32
BF16 = mybir.dt.bfloat16
AF = mybir.ActivationFunctionType
OP = mybir.AluOpType
AXX = mybir.AxisListType.X

D = 1024
T = 4096
QT = 256
NSLOT = 8
NPROF = [4, 8, 12, 16, 20, 24, 28, 32]
TILES = {0: [0, 3, 4, 7, 8, 11, 12, 15],
         1: [1, 2, 5, 6, 9, 10, 13, 14]}
SCALE = float(6.0 ** -0.5)
QPAIRS = [(0, 1), (0, 2), (0, 3), (1, 2), (1, 3), (2, 3)]
KPAIRS = [(2, 3), (3, 1), (1, 2), (0, 3), (2, 0), (0, 1)]
NT_K = 32
FF = 4096
RK = 256
ONE_TENSOR_SLOTS = 5


def build_nc(stage_limit=99, slot_limit=8):
    nc = bass.Bass("TRN2", target_bir_lowering=False,
                   detect_race_conditions=False,
                   dynamic_dma_scratch_size=1024)

    d_x = nc.dram_tensor("x", [T, D], F32, kind="ExternalInput")
    d_xq = nc.dram_tensor("xq", [NSLOT * QT, D], F32, kind="ExternalInput")
    d_masks = nc.dram_tensor("masks", [NSLOT, 4, 128, QT], BF16, kind="ExternalInput")
    d_ident = nc.dram_tensor("ident", [128, 128], BF16, kind="ExternalInput")
    d_wq = nc.dram_tensor("wq", [D, 128], F32, kind="ExternalInput")
    d_wk = nc.dram_tensor("wk", [D, 128], F32, kind="ExternalInput")
    d_wv = nc.dram_tensor("wv", [D, D], F32, kind="ExternalInput")
    d_bv = nc.dram_tensor("bv", [1, D], F32, kind="ExternalInput")
    d_wout = nc.dram_tensor("wout", [D, D], F32, kind="ExternalInput")
    d_bout = nc.dram_tensor("bout", [128, 8], F32, kind="ExternalInput")
    d_u1 = nc.dram_tensor("u1", [D, RK], F32, kind="ExternalInput")
    d_v1 = nc.dram_tensor("v1", [RK, FF], F32, kind="ExternalInput")
    d_b1 = nc.dram_tensor("b1", [128, 32], F32, kind="ExternalInput")
    d_u2 = nc.dram_tensor("u2", [FF, RK], F32, kind="ExternalInput")
    d_v2 = nc.dram_tensor("v2", [RK, D], F32, kind="ExternalInput")
    d_b2 = nc.dram_tensor("b2", [128, 8], F32, kind="ExternalInput")
    d_y = nc.dram_tensor("y", [NSLOT * QT, D], F32, kind="ExternalOutput")
    DBG = bool(_os.environ.get("K_DBG"))
    d_jlk = nc.dram_tensor("jlk_s", [4, 128, T], BF16,
                           kind="ExternalOutput" if DBG else "Internal")
    d_dbg = nc.dram_tensor("dbg", [128, 40960], F32, kind="ExternalOutput") if DBG else None
    d_wouts = nc.dram_tensor("wout_s", [8, 128, 1024], BF16, kind="Internal")
    d_u1s = nc.dram_tensor("u1_s", [2, 128, 1024], BF16, kind="Internal")
    d_v1s = nc.dram_tensor("v1_s", [4, 128, 2048], BF16, kind="Internal")
    d_u2s = nc.dram_tensor("u2_s", [2, 128, 4096], BF16, kind="Internal")
    d_v2s = nc.dram_tensor("v2_s", [1, 128, 2048], BF16, kind="Internal")

    steps = {k: [] for k in ("sync", "scalar", "vector", "tensor", "gpsimd")}
    floors = {k: {} for k in steps}

    class Sem:
        __slots__ = ("name", "h", "n")

        def __init__(self, name):
            self.name, self.h, self.n = name, None, 0

    sems = []

    def mksem(name):
        s = Sem(name)
        sems.append(s)
        return s

    dL = mksem("dL")
    dO = mksem("dO")
    dS = mksem("dS")
    cV = mksem("cV")
    cA = mksem("cA")
    cT = mksem("cT")
    cG = mksem("cG")
    cSC = mksem("cSC")
    cEX = mksem("cEX")
    cMK = mksem("cMK")
    cAV = mksem("cAV")

    def em(eng, f):
        steps[eng].append(f)

    def W(eng, sem, v):
        if v <= 0:
            return
        if floors[eng].get(sem.name, 0) >= v:
            return
        floors[eng][sem.name] = v
        em(eng, lambda e, s=sem, vv=v: e.wait_ge(s.h, vv))

    def I(eng, build, incs=(), waits=()):
        for (sem, v) in waits:
            W(eng, sem, v)
        for (sem, k) in incs:
            sem.n += k

        assert len(incs) <= 1, "one inc per instruction"

        def run(e, b=build, incs=tuple(incs)):
            inst = b(e)
            if incs:
                inst.then_inc(incs[0][0].h, incs[0][1])
        em(eng, run)

    def dma(qeng, qsem, out_ap, in_ap, waits=()):
        I(qeng, lambda e, o=out_ap, i=in_ap: e.dma_start(o, i),
          incs=[(qsem, 16)], waits=waits)

    guards = []

    def sb(name, shape, dtype):
        g = nc.sbuf_tensor(name, shape, dtype)
        t = g.__enter__()
        guards.append(g)
        return t

    def psum(name, shape, dtype):
        g = nc.psum_tensor(name, shape, dtype)
        t = g.__enter__()
        guards.append(g)
        return t

    ps_s0 = psum("ps_s0", [128, 1024], F32)
    ps_s1 = psum("ps_s1", [128, 1024], F32)
    ps_nAB = psum("ps_nAB", [128, 512], F32)
    ps_nCD = psum("ps_nCD", [128, 512], F32)
    ps_den = psum("ps_den", [128, 512], F32)
    ps_u = psum("ps_u", [128, 512], F32)

    v_sb = sb("v_sb", [128, NT_K * 1024], BF16)
    lq = [sb(f"lq{G}", [128, NSLOT * QT], BF16) for G in range(4)]
    mbuf = [sb(f"mbuf{i}", [128, 4 * QT], BF16) for i in range(2)]
    p_sb = [sb(f"p{j}", [128, 1024], BF16) for j in range(8)]
    jbuf = [sb(f"jbuf{i}", [128, 4096], BF16) for i in range(2)]
    wbuf = [sb(f"wbuf{i}", [128, 4096], BF16) for i in range(2)]
    wv_sb = sb("wv_sb", [128, 8 * 1024], BF16)
    wq_sb = sb("wq_sb", [128, 8 * 128], BF16)
    wk_sb = sb("wk_sb", [128, 8 * 128], BF16)
    ao_sb = sb("ao_sb", [128, 8 * QT], BF16)
    h_sb = sb("h_sb", [128, 32 * QT], BF16)
    xin = sb("xin", [128, D], F32)
    xnb = sb("xnb", [128, D], BF16)
    lnxt = sb("lnxt", [128, 8 * 128], BF16)
    ln2t = sb("ln2t", [128, 8 * QT], BF16)
    tmbf = [sb(f"tmbf{i}", [128, 512], BF16) for i in range(2)]
    jstage = sb("jstage", [128, 512], BF16)
    e_t1 = sb("e_t1", [128, 128], F32)
    e_t2 = sb("e_t2", [128, 128], F32)
    e_sq = sb("e_sq", [128, 96], F32)
    e_ss = sb("e_ss", [128, 16], F32)
    e_rs = sb("e_rs", [128, 16], F32)
    k12 = sb("k12", [128, 128], F32)
    x2_sb = sb("x2_sb", [128, 2 * D], F32)
    yt_sb = sb("yt_sb", [128, 2 * D], F32)
    ytp = sb("ytp", [128, 256], F32)
    p1_sb = sb("p1_sb", [128, 2 * QT], BF16)
    q2_sb = sb("q2_sb", [128, 2 * QT], BF16)
    stage = [h_sb.ap().bitcast(F32)[:, 0:1024],
             h_sb.ap().bitcast(F32)[:, 1024:2048]]
    bstage = [h_sb.ap()[:, 4096:5120], h_sb.ap()[:, 5120:6144]]
    xqin = sb("xqin", [128, D], F32)
    bn_sb = sb("bn_sb", [128, 12], F32)
    mv_sb = sb("mv_sb", [128, 2], F32)
    rstd_sb = sb("rstd_sb", [128, 1], F32)
    eps_sb = sb("eps_sb", [128, 1], F32)
    eps24_sb = sb("eps24_sb", [128, 1], F32)
    ident_sb = sb("ident_sb", [128, 128], BF16)
    identf_sb = sb("identf_sb", [128, 128], F32)
    ones_bf = sb("ones_bf", [128, 1], BF16)
    onesr = sb("onesr", [128, 128], BF16)
    bv_sb = sb("bv_sb", [1, D], BF16)
    b1_sb = sb("b1_sb", [128, 32], F32)
    bout_sb = sb("bout_sb", [128, 8], F32)
    b2_sb = sb("b2_sb", [128, 8], F32)
    spc_sb = sb("spc", [128, 4], F32)
    rb_sb = sb("rb_sb", [128, 512], F32)
    rd_sb = sb("rd_sb", [128, 1024], BF16)

    wv_view = wv_sb.ap().rearrange("p (c n) -> p c n", c=8)
    s1bf = ps_s1.ap().bitcast(BF16)
    ubf = ps_u.ap().bitcast(BF16)

    def mm(out, lhsT, rhs, start=True, stop=True, tp=None, incs=(), waits=()):
        I("tensor",
          lambda e, o=out, l=lhsT, r=rhs, st=start, sp=stop, tpp=tp:
          e.matmul(o, l, r, start=st, stop=sp, tile_position=tpp,
                   skip_group_check=True),
          incs=incs, waits=waits)

    def tr(out, in_, incs=(), waits=()):
        I("tensor",
          lambda e, o=out, i=in_, idn=(identf_sb if in_.dtype == F32 else ident_sb):
          e.transpose(o, i, idn[:]),
          incs=incs, waits=waits)

    def act(out, in_, func, bias=0.0, scale=1.0, incs=(), waits=()):
        I("scalar",
          lambda e, o=out, i=in_, f=func, b=bias, s=scale:
          e.activation(o, i, f, bias=b, scale=s),
          incs=incs, waits=waits)

    guard = {"xin": 0, "xnb": 0, "tmbf0": 0, "tmbf1": 0, "ao": 0}

    def vfence():
        W("vector", cV, cV.n)

    def ln_tile(src_ap, dst_ap, dst_guard_T=0):
        I("vector", lambda e, s=src_ap: e.bn_stats(bn_sb[:, 0:6], s[:, 0:512]))
        I("vector", lambda e, s=src_ap: e.bn_stats(bn_sb[:, 6:12], s[:, 512:1024]),
          incs=[(cV, 1)])
        vfence()
        I("vector", lambda e: e.bn_aggr(mv_sb[:], bn_sb[:]), incs=[(cV, 1)])
        act(rstd_sb[:], mv_sb[:, 1:2], AF.Ln, bias=eps_sb[:], waits=[(cV, cV.n)])
        act(rstd_sb[:], rstd_sb[:], AF.Exp, scale=-0.5, incs=[(cA, 1)])
        I("vector",
          lambda e, s=src_ap, d=dst_ap: e.tensor_scalar(
              d, s, mv_sb[:, 0:1], rstd_sb[:], op0=OP.subtract, op1=OP.mult),
          incs=[(cV, 1)], waits=[(cA, cA.n), (cT, dst_guard_T)])

    def exterior(src_ap, pairs, dst_ap, dst_guard_T=0):
        def v1c(c):
            return src_ap[:, c * 16:c * 16 + 16].rearrange(
                "p (g a) -> p g a", g=4).unsqueeze(3)

        def v2c(c):
            return src_ap[:, 64 + c * 16:64 + c * 16 + 16].rearrange(
                "p (g a) -> p g a", g=4).unsqueeze(3)

        t1v = e_t1.ap().rearrange("p (g a x) -> p g a x", g=4, a=4, x=8)
        t2v = e_t2.ap().rearrange("p (g a x) -> p g a x", g=4, a=4, x=8)
        vfence()
        for p, (a, b) in enumerate(pairs):
            I("vector", lambda e, o=t1v[:, :, :, p:p + 1], x=v1c(a), y=v2c(b):
              e.tensor_tensor(o, x, y, op=OP.mult))
            I("vector", lambda e, o=t2v[:, :, :, p:p + 1], x=v1c(b), y=v2c(a):
              e.tensor_tensor(o, x, y, op=OP.mult))
        Lsp = t1v[:, :, :, 0:6]
        I("vector", lambda e, L=Lsp: e.tensor_tensor(L, L, t2v[:, :, :, 0:6],
                                                     op=OP.subtract))
        sqv = e_sq.ap().rearrange("p (g a x) -> p g a x", g=4, a=4, x=6)
        I("vector", lambda e, L=Lsp, s=sqv: e.tensor_tensor(s, L, L, op=OP.mult))
        I("vector", lambda e, s=sqv: e.reduce_sum(
            e_ss.ap().rearrange("p (g a) -> p g a", g=4).unsqueeze(3),
            s, axis=AXX), incs=[(cV, 1)])
        act(e_rs[:], e_ss[:], AF.Ln, bias=eps24_sb[:], waits=[(cV, cV.n)])
        act(e_rs[:], e_rs[:], AF.Exp, scale=-0.5, incs=[(cA, 1)])
        rsb = e_rs.ap().rearrange("p (g a) -> p g a", g=4).unsqueeze(3) \
            .broadcast_to([128, 4, 4, 6])
        I("vector", lambda e, d=dst_ap, L=Lsp, r=rsb: e.tensor_tensor(
            d.rearrange("p (g a x) -> p g a x", g=4, a=4)[:, :, :, 0:6],
            L, r, op=OP.mult),
          incs=[(cV, 1)], waits=[(cA, cA.n), (cT, dst_guard_T)])

    castv = [0, 0]
    bst_free = [0, 0]

    def spill(piece_idx, in_ap, out_ap, to_sbuf=None):
        b = piece_idx % 2
        dma("scalar", dS, stage[b][:, 0:in_ap.size() // 128], in_ap,
            waits=[(cV, castv[b])])
        dv = dS.n
        w = out_ap.size() // 128 if out_ap is not None else to_sbuf.size() // 128
        if to_sbuf is not None:
            I("vector", lambda e, o=to_sbuf, w=w: e.tensor_copy(
                o, stage[b][:, 0:w]), incs=[(cV, 1)], waits=[(dS, dv)])
            castv[b] = cV.n
        else:
            I("vector", lambda e, w=w: e.tensor_copy(
                bstage[b][:, 0:w], stage[b][:, 0:w]),
              incs=[(cV, 1)], waits=[(dS, dv)])
            castv[b] = cV.n
            dma("scalar", dS, out_ap, bstage[b].rearrange(
                "p (a n) -> p a n", a=out_ap.shape[-2] if out_ap.ndim == 3 else 1)
                if False else bstage[b][:, 0:w],
                waits=[(cV, cV.n)])

    def startup():
        I("vector", lambda e: e.memset(eps_sb[:], 1e-5))
        I("vector", lambda e: e.memset(eps24_sb[:], 1e-24))
        I("vector", lambda e: e.memset(ones_bf[:], 1.0))
        I("vector", lambda e: e.memset(onesr[:], 1.0))
        I("vector", lambda e: e.memset(e_t1[:], 0.0))
        I("vector", lambda e: e.memset(e_t2[:], 0.0))
        I("vector", lambda e: e.memset(tmbf[0][:], 0.0))
        I("vector", lambda e: e.memset(tmbf[1][:], 0.0), incs=[(cV, 1)])
        dma("sync", dL, ident_sb[:], d_ident.ap())
        I("vector", lambda e: e.tensor_copy(identf_sb[:], ident_sb[:]),
          incs=[(cV, 1)], waits=[(dL, dL.n)])

        dma("sync", dL, b1_sb[:], d_b1.ap())
        dma("sync", dL, bout_sb[:], d_bout.ap())
        dma("sync", dL, b2_sb[:], d_b2.ap())
        pi = 0
        spill(pi, d_wq.ap().rearrange("(r p) n -> p r n", p=128), None,
              to_sbuf=wq_sb[:]); pi += 1
        spill(pi, d_wk.ap().rearrange("(r p) n -> p r n", p=128), None,
              to_sbuf=wk_sb[:]); pi += 1
        dma("scalar", dS, stage[0][0:1, :], d_bv.ap(), waits=[(cV, castv[0])])
        I("vector", lambda e: e.tensor_copy(bv_sb[:], stage[0][0:1, :]),
          incs=[(cV, 1)], waits=[(dS, dS.n)])
        castv[0] = cV.n
        for r in range(8):
            spill(pi, d_wv.ap()[r * 128:(r + 1) * 128, :], None,
                  to_sbuf=wv_sb[:, r * 1024:(r + 1) * 1024]); pi += 1
        for j in range(8):
            spill(pi, d_wout.ap().rearrange("(r p) n -> p r n", p=128)
                  [:, :, j * 128:(j + 1) * 128], d_wouts.ap()[j]); pi += 1
        for m in range(2):
            spill(pi, d_u1.ap().rearrange("(r p) n -> p r n", p=128)
                  [:, :, m * 128:(m + 1) * 128], d_u1s.ap()[m]); pi += 1
        for q in range(4):
            for rh in range(2):
                spill(pi, d_v1.ap()[rh * 128:(rh + 1) * 128,
                                    q * 1024:(q + 1) * 1024],
                      d_v1s.ap()[q].rearrange("p (c x n) -> p c x n", c=8, x=2)
                      [:, :, rh, :]); pi += 1
        for m in range(2):
            for cg in range(4):
                spill(pi, d_u2.ap().rearrange("(c p) n -> p c n", p=128)
                      [:, cg * 8:(cg + 1) * 8, m * 128:(m + 1) * 128],
                      d_u2s.ap()[m][:, cg * 1024:(cg + 1) * 1024]); pi += 1
        for m in range(2):
            spill(pi, d_v2.ap()[m * 128:(m + 1) * 128, :],
                  d_v2s.ap()[0].rearrange("p (j x n) -> p j x n", j=8, x=2)
                  [:, :, m, :]); pi += 1

    ph1V = []
    ph1J = []

    def ph1_tile(t):
        dma("sync", dL, xin[:], d_x.ap()[t * 128:(t + 1) * 128, :],
            waits=[(cV, guard["xin"])])
        W("vector", dL, dL.n)
        ln_tile(xin.ap(), xnb.ap(), dst_guard_T=guard["xnb"])
        guard["xin"] = cV.n
        lnv = cV.n
        for r in range(8):
            tr(s1bf[:, r * 128:(r + 1) * 128], xnb[:, r * 128:(r + 1) * 128],
               waits=([(cV, lnv), (cA, cA.n)] if r == 0 else ()),
               incs=([(cT, 1)] if r == 7 else ()))
        guard["xnb"] = cT.n
        I("vector", lambda e: e.tensor_copy(lnxt[:], s1bf[:, 0:1024]),
          incs=[(cV, 1)], waits=[(cT, cT.n)])
        lnxtv = cV.n
        for hh in range(2):
            dst = ps_s1[:, hh * 512:(hh + 1) * 512]
            for r in range(8):
                mm(dst, lnxt[:, r * 128:(r + 1) * 128],
                   wv_view[:, r, hh * 512:(hh + 1) * 512],
                   start=(r == 0), stop=False,
                   waits=([(cV, lnxtv), (cA, cA.n)] if r == 0 else ()))
            mm(dst, onesr[0:1, :], bv_sb[0:1, hh * 512:(hh + 1) * 512],
               start=False, stop=True, incs=([(cT, 1)] if hh == 1 else ()))
        act(v_sb[:, t * 1024:(t + 1) * 1024], ps_s1[:], AF.Copy,
            incs=[(cA, 1)], waits=[(cT, cT.n)])
        ph1V.append(cA.n)
        for r in range(8):
            mm(ps_s1[:, 0:128], lnxt[:, r * 128:(r + 1) * 128],
               wk_sb[:, r * 128:(r + 1) * 128],
               start=(r == 0), stop=(r == 7),
               waits=([(cA, cA.n)] if r == 0 else ()),
               incs=([(cT, 1)] if r == 7 else ()))
        I("vector", lambda e: e.tensor_copy(k12[:], ps_s1[:, 0:128]),
          incs=[(cV, 1)], waits=[(cT, cT.n)])
        buf = t % 2
        exterior(k12.ap(), KPAIRS, tmbf[buf].ap(),
                 dst_guard_T=guard[f"tmbf{buf}"])
        extv = cV.n
        for G in range(4):
            tr(s1bf[:, 1024 + G * 128:1024 + (G + 1) * 128],
               tmbf[buf][:, G * 128:(G + 1) * 128],
               waits=([(cV, extv), (cA, cA.n)] if G == 0 else ()),
               incs=([(cT, 1)] if G == 3 else ()))
        guard[f"tmbf{buf}"] = cT.n
        I("vector", lambda e: e.tensor_copy(jstage[:], s1bf[:, 1024:1536]),
          incs=[(cV, 1)], waits=[(cT, cT.n), (dO, (ph1J[-1] if ph1J else 0))])
        dma("sync", dO,
            d_jlk.ap()[:, :, t * 128:(t + 1) * 128].transpose([1, 0, 2]),
            jstage.ap().rearrange("p (g n) -> p g n", g=4),
            waits=[(cV, cV.n)])
        ph1J.append(dO.n)

    ph1Q = [0] * NSLOT

    def ph1b_slot(si):
        for tt in range(2):
            tq = si * 2 + tt
            dma("sync", dL, xin[:], d_xq.ap()[tq * 128:(tq + 1) * 128, :],
                waits=[(cV, guard["xin"])])
            W("vector", dL, dL.n)
            ln_tile(xin.ap(), xnb.ap(), dst_guard_T=guard["xnb"])
            guard["xin"] = cV.n
            lnv = cV.n
            for r in range(8):
                tr(s1bf[:, r * 128:(r + 1) * 128], xnb[:, r * 128:(r + 1) * 128],
                   waits=([(cV, lnv), (cA, cA.n)] if r == 0 else ()),
                   incs=([(cT, 1)] if r == 7 else ()))
            guard["xnb"] = cT.n
            I("vector", lambda e: e.tensor_copy(lnxt[:], s1bf[:, 0:1024]),
              incs=[(cV, 1)], waits=[(cT, cT.n)])
            lnxv = cV.n
            for r in range(8):
                mm(ps_s1[:, 0:128], lnxt[:, r * 128:(r + 1) * 128],
                   wq_sb[:, r * 128:(r + 1) * 128],
                   start=(r == 0), stop=(r == 7),
                   waits=([(cV, lnxv), (cA, cA.n)] if r == 0 else ()),
                   incs=([(cT, 1)] if r == 7 else ()))
            I("vector", lambda e: e.tensor_copy(k12[:], ps_s1[:, 0:128]),
              incs=[(cV, 1)], waits=[(cT, cT.n)])
            buf = tt
            exterior(k12.ap(), QPAIRS, tmbf[buf].ap(),
                     dst_guard_T=guard[f"tmbf{buf}"])
            extv = cV.n
            for G in range(4):
                tr(s1bf[:, 1024 + G * 128:1024 + (G + 1) * 128],
                   tmbf[buf][:, G * 128:(G + 1) * 128],
                   waits=([(cV, extv), (cA, cA.n)] if G == 0 else ()),
                   incs=([(cT, 1)] if G == 3 else ()))
            guard[f"tmbf{buf}"] = cT.n
            for G in range(4):
                I("vector", lambda e, G=G, si=si, tt=tt: e.tensor_copy(
                    lq[G][:, si * QT + tt * 128: si * QT + (tt + 1) * 128],
                    s1bf[:, 1024 + G * 128:1024 + (G + 1) * 128]),
                  incs=([(cV, 1)] if G == 3 else ()), waits=[(cT, cT.n)])
        ph1Q[si] = cV.n

    ggi = [0]
    quad_no = [0]
    s_free = {0: 0, 1: 0}
    jbuf_free = [0, 0]
    jd = {}
    exp_after = {}
    mask_after = {}

    mload = [0] * NSLOT

    def mask_load(si):
        dma("sync", dL, mbuf[si % 2].ap().rearrange("p (j q) -> p j q", j=4),
            d_masks.ap()[si].transpose([1, 0, 2]),
            waits=[(cMK, mbuf_free[si % 2])])
        mload[si] = dL.n

    mbuf_free = [0, 0]

    def jlk_prefetch(si, G):
        n = NPROF[si]
        b = G % 2
        dma("sync", dL, jbuf[b][:, 0:n * 128], d_jlk.ap()[G][:, 0:n * 128],
            waits=[(cSC, jbuf_free[b]), (dO, ph1J[n - 1])])
        jd[(si, G)] = dL.n

    def emit_av(si, G, my_quad, gj, h_ggi, chunks, masked, ngrp):
        slabs = [p_sb[(h_ggi % 2) * 4 + a] for a in range(4)]
        first = gj == 0
        last = gj == ngrp - 1
        for ci, c in enumerate(chunks):
            w = []
            if ci == 0:
                for a in range(4):
                    if masked:
                        w.append((cMK, mask_after[(h_ggi, a)]))
                    else:
                        w.append((cEX, exp_after[(h_ggi, a)]))
                w.append((cA, ph1V[max(chunks)]))
                if first and my_quad >= 1:
                    w.append((cV, ev_vals[my_quad - 1]))
            st = first and ci == 0
            sp = last and ci == 3
            for a in range(4):
                nb = ps_nAB if a < 2 else ps_nCD
                vcol = c * 1024 + (4 * G + a) * 64
                base = 64 * (a % 2)
                mm(nb[base:base + 64, 0:256],
                   v_sb[:, vcol:vcol + 64],
                   slabs[a][:, ci * 256:(ci + 1) * 256],
                   start=st, stop=sp, tp=(0, base),
                   waits=(w if a == 0 else ()))
            dendst = [ps_nAB[0:1, 256:512], ps_den[0:1, 0:256],
                      ps_nCD[0:1, 256:512], ps_den[0:1, 256:512]]
            for a in range(4):
                mm(dendst[a], ones_bf[:],
                   slabs[a][:, ci * 256:(ci + 1) * 256],
                   start=(st if a == 1 else False), stop=sp, tp=(0, 0),
                   incs=([(cAV, 1)] if (a == 3 and ci == 3) else ()))

    def attn_quad(si, G):
        n = NPROF[si]
        ngrp = n // 4
        two_t = si >= ONE_TENSOR_SLOTS
        grp = [list(range(n - 4, n))] + \
              [list(range(4 * g, 4 * g + 4)) for g in range(ngrp - 1)]
        qv = ph1Q[si]
        my_quad = quad_no[0]
        quad_no[0] += 1
        jdv = jd[(si, G)]
        pend = []
        for gi, chunks in enumerate(grp):
            masked = gi == 0
            g_ggi = ggi[0]
            ggi[0] += 1
            slabs = [p_sb[(g_ggi % 2) * 4 + a] for a in range(4)]
            ring_wait = max(0, g_ggi - 1)
            sc_done = {}

            def sc_head(a, st):
                stv = ps_s0 if st == 0 else ps_s1
                for ci, c in enumerate(chunks):
                    mm(stv[:, ci * 256:(ci + 1) * 256],
                       jbuf[G % 2][32 * a:32 * a + 6, c * 128:(c + 1) * 128],
                       lq[G][32 * a:32 * a + 6, si * QT:(si + 1) * QT],
                       tp=(32 * a, 0),
                       waits=([(dL, jdv), (cV, qv), (cEX, s_free[st])]
                              if ci == 0 else ()),
                       incs=([(cSC, 1)] if ci == 3 else ()))
                sc_done[a] = cSC.n

            def exp_head(a, st):
                stv = ps_s0 if st == 0 else ps_s1
                act(slabs[a][:], stv[:], AF.Exp, scale=SCALE,
                    waits=[(cSC, sc_done[a]), (cAV, ring_wait)],
                    incs=[(cEX, 1)])
                exp_after[(g_ggi, a)] = cEX.n
                s_free[st] = cEX.n

            if two_t:
                sc_head(0, 0); sc_head(2, 1)
                exp_head(0, 0); exp_head(2, 1)
                sc_head(1, 0); sc_head(3, 1)
                exp_head(1, 0); exp_head(3, 1)
            else:
                for a in range(4):
                    sc_head(a, 0)
                    exp_head(a, 0)
            if masked:
                for a in range(4):
                    I("vector", lambda e, sl=slabs[a], si=si: e.tensor_tensor(
                        sl[:], sl[:], mbuf[si % 2][:], op=OP.mult),
                      waits=[(cEX, exp_after[(g_ggi, a)]), (dL, mload[si])],
                      incs=[(cMK, 1)])
                    mask_after[(g_ggi, a)] = cMK.n
            pend.append((gi, g_ggi, chunks, masked))
            if gi >= 1:
                gj, h, chs, msk = pend.pop(0)
                emit_av(si, G, my_quad, gj, h, chs, msk, ngrp)
        for (gj, h, chs, msk) in pend:
            emit_av(si, G, my_quad, gj, h, chs, msk, ngrp)

    def attn_tail(si, G):
        avv = ggi[0]
        densrc = [ps_nAB[0:1, 256:512], ps_den[0:1, 0:256],
                  ps_nCD[0:1, 256:512], ps_den[0:1, 256:512]]
        def _recip(e, a, s):
            with nc.allow_low_precision(reason="recip cast to bf16 for matmul"
                                        " rhs; same precision as prior"
                                        " fp32-recip + bf16-copy path"):
                return e.reciprocal(rd_sb[0:1, 256 * a:256 * (a + 1)], s)
        for a in range(4):
            I("vector", lambda e, a=a, s=densrc[a]: _recip(e, a, s),
              waits=([(cAV, avv)] if a == 0 else ()),
              incs=([(cV, 1)] if a == 3 else ()))
        rdv = cV.n
        bc = [(0, ps_u[0:32, 0:256], (0, 0)), (0, ps_u[32:64, 0:256], (0, 32)),
              (1, ps_u[64:96, 0:256], (0, 64)), (1, ps_u[96:128, 0:256], (0, 96)),
              (2, ps_u[0:32, 256:512], (0, 0)), (2, ps_u[32:64, 256:512], (0, 32)),
              (3, ps_u[64:96, 256:512], (0, 64)), (3, ps_u[96:128, 256:512], (0, 96))]
        for i, (a, dst, tpp) in enumerate(bc):
            mm(dst, onesr[0:1, 0:32], rd_sb[0:1, 256 * a:256 * (a + 1)],
               tp=tpp,
               waits=([(cV, max(rdv, b7V[0])), (cA, b7A[0])] if i == 0 else ()),
               incs=([(cT, 1)] if i == 7 else ()))
        bcv = cT.n
        I("vector", lambda e: e.tensor_copy(rb_sb[:], ps_u[:]),
          waits=[(cT, bcv)], incs=[(cV, 1)])
        b7V[0] = cV.n
        for a in range(4):
            h = 4 * G + a
            nsrc = ps_nAB if a < 2 else ps_nCD
            base = 64 * (a % 2)
            ucol = 0 if a < 2 else 256
            I("vector", lambda e, h=h, nsrc=nsrc, base=base, ucol=ucol:
              e.tensor_tensor(
                  ao_sb[64 * (h % 2):64 * (h % 2) + 64,
                        (h // 2) * QT:(h // 2 + 1) * QT],
                  nsrc[base:base + 64, 0:256],
                  rb_sb[base:base + 64, ucol:ucol + 256], op=OP.mult),
              waits=[(cT, guard["ao"])],
              incs=([(cV, 1)] if a == 3 else ()))
        ev_vals.append(cV.n)

    wbuf_free = [0, 0]
    ev_vals = []
    b7A = [0]
    b7V = [0]
    wpc = [0]
    h_guard = [0]
    h_acts = {}
    out_dma = [0]
    spill_done = [0]

    def wpiece(dram_ap, ncols):
        b = wpc[0] % 2
        wpc[0] += 1
        dma("scalar", dS, wbuf[b][:, 0:ncols], dram_ap,
            waits=[(cT, wbuf_free[b]), (dS, spill_done[0])])
        return b, dS.n

    def piece_P0(si):
        for j in range(8):
            b, dv = wpiece(d_wouts.ap()[j], 1024)
            for rr in range(8):
                mm(ps_u[:, 0:256], wbuf[b][:, rr * 128:(rr + 1) * 128],
                   ao_sb[:, rr * QT:(rr + 1) * QT],
                   start=(rr == 0), stop=(rr == 7),
                   waits=([(dS, dv), (cV, ev_vals[4 * (si + 1) - 1]),
                           (cA, b7A[0]), (cV, b7V[0])]
                          if rr == 0 else ()),
                   incs=([(cT, 1)] if rr == 7 else ()))
            wbuf_free[b] = cT.n
            act(ytp[:], ps_u[:, 0:256], AF.Identity, bias=bout_sb[:, j:j + 1],
                incs=[(cA, 1)], waits=[(cT, cT.n)])
            b7A[0] = cA.n
            for tt in range(2):
                tr(ps_u[:, tt * 128:(tt + 1) * 128], ytp[:, tt * 128:(tt + 1) * 128],
                   waits=([(cA, cA.n)] if tt == 0 else ()),
                   incs=([(cT, 1)] if tt == 1 else ()))
            I("vector", lambda e, j=j: e.tensor_copy(
                yt_sb.ap().rearrange("p (t n) -> p t n", t=2)
                [:, :, j * 128:(j + 1) * 128],
                ps_u.ap()[:, 0:256].rearrange("p (t n) -> p t n", t=2)),
              incs=[(cV, 1)], waits=[(cT, cT.n)])
            b7V[0] = cV.n

    def piece_P1(si):
        for tt in range(2):
            dma("sync", dL, xqin[:], d_xq.ap()[(si * 2 + tt) * 128:
                                               (si * 2 + tt + 1) * 128, :],
                waits=[(cG, cG.n)])
            ldv = dL.n
            yv = yt_sb.ap()[:, tt * 1024:(tt + 1) * 1024]
            ln_tile(yv, yv)
            lnv = cV.n
            I("gpsimd", lambda e, tt=tt, yv=yv: e.tensor_add(
                x2_sb[:, tt * 1024:(tt + 1) * 1024], xqin[:], yv),
              waits=[(cV, lnv), (dL, ldv), (dO, out_dma[0])], incs=[(cG, 1)])
            W("vector", cG, cG.n)
            ln_tile(x2_sb.ap()[:, tt * 1024:(tt + 1) * 1024], xnb.ap(),
                    dst_guard_T=guard["xnb"])
            lnv2 = cV.n
            for half in range(2):
                for r in range(4):
                    tr(ubf[:, half * 512 + r * 128: half * 512 + (r + 1) * 128],
                       xnb[:, half * 512 + r * 128: half * 512 + (r + 1) * 128],
                       waits=([(cV, max(lnv2, b7V[0])), (cA, b7A[0])]
                              if r == 0 else ()),
                       incs=([(cT, 1)] if r == 3 else ()))
                if half == 1:
                    guard["xnb"] = cT.n
                I("vector", lambda e, tt=tt, half=half: e.tensor_copy(
                    ln2t.ap().rearrange("p (r q) -> p r q", r=8)
                    [:, half * 4:(half + 1) * 4, tt * 128:(tt + 1) * 128],
                    ubf.rearrange("p (r q) -> p r q", r=8)
                    [:, half * 4:(half + 1) * 4, :]),
                  incs=[(cV, 1)], waits=[(cT, cT.n)])
                b7V[0] = cV.n

    def piece_P2(si):
        l2v = cV.n
        for m in range(2):
            b, dv = wpiece(d_u1s.ap()[m], 1024)
            for rr in range(8):
                mm(ps_u[:, 0:256], wbuf[b][:, rr * 128:(rr + 1) * 128],
                   ln2t[:, rr * QT:(rr + 1) * QT],
                   start=(rr == 0), stop=(rr == 7),
                   waits=([(dS, dv), (cV, max(l2v, b7V[0])), (cA, b7A[0])]
                          if rr == 0 else ()),
                   incs=([(cT, 1)] if rr == 7 else ()))
            wbuf_free[b] = cT.n
            act(p1_sb[:, m * QT:(m + 1) * QT], ps_u[:, 0:256], AF.Copy,
                incs=[(cA, 1)], waits=[(cT, cT.n)])
            b7A[0] = cA.n
        p1v = cA.n
        for q in range(4):
            b, dv = wpiece(d_v1s.ap()[q], 2048)
            for cc8 in range(8):
                cc = q * 8 + cc8
                for rh in range(2):
                    mm(ps_u[:, 0:256],
                       wbuf[b][:, cc8 * 256 + rh * 128: cc8 * 256 + (rh + 1) * 128],
                       p1_sb[:, rh * QT:(rh + 1) * QT],
                       start=(rh == 0), stop=(rh == 1),
                       waits=([(dS, dv), (cA, max(p1v, b7A[0])), (cV, b7V[0])]
                              if rh == 0 else ()),
                       incs=([(cT, 1)] if rh == 1 else ()))
                act(h_sb[:, cc * QT:(cc + 1) * QT], ps_u[:, 0:256],
                    AF.Gelu_apprx_tanh, bias=b1_sb[:, cc:cc + 1],
                    incs=[(cA, 1)],
                    waits=[(cT, cT.n), (cT, h_guard[0])])
                b7A[0] = cA.n
                h_acts[cc] = cA.n
            wbuf_free[b] = cT.n

    def piece_P3(si):
        for m in range(2):
            b, dv = wpiece(d_u2s.ap()[m], 4096)
            for cc in range(32):
                w = [(cA, h_acts[cc])] if m == 0 else []
                if cc == 0:
                    w += [(dS, dv), (cA, b7A[0]), (cV, b7V[0])]
                mm(ps_u[:, 0:256], wbuf[b][:, cc * 128:(cc + 1) * 128],
                   h_sb[:, cc * QT:(cc + 1) * QT],
                   start=(cc == 0), stop=(cc == 31),
                   waits=w, incs=([(cT, 1)] if cc == 31 else ()))
            wbuf_free[b] = cT.n
            act(q2_sb[:, m * QT:(m + 1) * QT], ps_u[:, 0:256], AF.Copy,
                incs=[(cA, 1)], waits=[(cT, cT.n)])
            b7A[0] = cA.n
        h_guard[0] = cT.n
        q2v = cA.n
        b, dv = wpiece(d_v2s.ap()[0], 2048)
        for j in range(8):
            for m in range(2):
                mm(ps_u[:, 0:256],
                   wbuf[b][:, j * 256 + m * 128: j * 256 + (m + 1) * 128],
                   q2_sb[:, m * QT:(m + 1) * QT],
                   start=(m == 0), stop=(m == 1),
                   waits=([(dS, dv), (cA, max(q2v, b7A[0])), (cV, b7V[0])]
                          if m == 0 else ()),
                   incs=([(cT, 1)] if m == 1 else ()))
            act(ytp[:], ps_u[:, 0:256], AF.Identity, bias=b2_sb[:, j:j + 1],
                incs=[(cA, 1)], waits=[(cT, cT.n)])
            b7A[0] = cA.n
            for tt in range(2):
                tr(ps_u[:, tt * 128:(tt + 1) * 128],
                   ytp[:, tt * 128:(tt + 1) * 128],
                   waits=([(cA, cA.n)] if tt == 0 else ()),
                   incs=([(cT, 1)] if tt == 1 else ()))
            I("vector", lambda e, j=j: e.tensor_copy(
                yt_sb.ap().rearrange("p (t n) -> p t n", t=2)
                [:, :, j * 128:(j + 1) * 128],
                ps_u.ap()[:, 0:256].rearrange("p (t n) -> p t n", t=2)),
              incs=[(cV, 1)], waits=[(cT, cT.n)])
            b7V[0] = cV.n
        wbuf_free[b] = cT.n
        for tt in range(2):
            yv = yt_sb.ap()[:, tt * 1024:(tt + 1) * 1024]
            ln_tile(yv, yv)
            lnv = cV.n
            I("gpsimd", lambda e, tt=tt, yv=yv: e.tensor_add(
                x2_sb[:, tt * 1024:(tt + 1) * 1024],
                x2_sb[:, tt * 1024:(tt + 1) * 1024], yv),
              waits=[(cV, lnv)], incs=[(cG, 1)])
            dma("sync", dO,
                d_y.ap()[(si * 2 + tt) * 128:(si * 2 + tt + 1) * 128, :],
                x2_sb[:, tt * 1024:(tt + 1) * 1024], waits=[(cG, cG.n)])
        out_dma[0] = dO.n

    pieces = [piece_P0, piece_P1, piece_P2, piece_P3]

    startup()
    spill_done[0] = dS.n
    W("tensor", dL, dL.n)
    if stage_limit < 1:
        return _realize(nc, sems, steps, dO)
    W("tensor", cV, cV.n)
    W("scalar", dL, dL.n)
    W("vector", dL, dL.n)
    W("gpsimd", dL, dL.n)
    ph1b_slot(0)
    ph1b_slot(1)
    kt = [0]
    if stage_limit < 2:
        return _realize(nc, sems, steps, dO)

    def emit_ph1_until(ntiles):
        while kt[0] < ntiles:
            ph1_tile(kt[0])
            kt[0] += 1

    if stage_limit < 3:
        for tq in range(2, 8):
            ph1b_slot(tq)
        emit_ph1_until(NT_K)
        if DBG:
            W("sync", cV, cV.n)
            W("sync", cA, cA.n)
            col = [0]

            def dump(ap, w):
                dma("sync", dO, d_dbg.ap()[:, col[0]:col[0] + w], ap)
                col[0] += w
            dump(v_sb.ap().bitcast(F32), 16384)
            dump(ao_sb.ap().bitcast(F32), 1024)
            for G in range(4):
                dump(lq[G].ap().bitcast(F32), 1024)
            W("sync", dO, dO.n)
            dma("sync", dL, jbuf[0][:], d_jlk.ap()[0])
            W("sync", dL, dL.n)
            dump(jbuf[0].ap().bitcast(F32), 2048)
        return _realize(nc, sems, steps, dO)
    ph1b_sched = {0: [2, 3], 1: [4, 5], 2: [6, 7]}
    for si in range(min(NSLOT, slot_limit)):
        emit_ph1_until(NPROF[si])
        mask_load(si)
        if si >= 2:
            mbuf_free[si % 2] = cMK.n
        if si == ONE_TENSOR_SLOTS:
            emit_ph1_until(NT_K)
            W("tensor", cA, ph1V[NT_K - 1])
            W("tensor", cV, cV.n)
        jlk_prefetch(si, 0)
        for G in range(4):
            if G < 3:
                pass
            attn_quad(si, G)
            jbuf_free[G % 2] = cSC.n
            if G < 3:
                jlk_prefetch(si, G + 1)
            if si > 0:
                pieces[G](si - 1)
                if G == 0:
                    guard["ao"] = cT.n
                if DBG and si == 1 and G == 0:
                    dma("sync", dO, d_dbg.ap()[:, 38912:39936],
                        ao_sb.ap().bitcast(F32), waits=[(cV, ev_vals[3])])
                    W("vector", dO, dO.n)
            elif si in ph1b_sched and G in (1, 3):
                ph1b_slot(ph1b_sched[si][0 if G == 1 else 1])
            attn_tail(si, G)
        if si in (1, 2) :
            for tq in ph1b_sched[si]:
                ph1b_slot(tq)
    if slot_limit >= NSLOT:
        for G in range(4):
            pieces[G](NSLOT - 1)
            if G == 0:
                guard["ao"] = cT.n
    if DBG:
        W("sync", cV, cV.n)
        W("sync", cA, cA.n)
        W("sync", cG, cG.n)
        col = [0]

        def dump(ap, w):
            dma("sync", dO, d_dbg.ap()[:, col[0]:col[0] + w], ap)
            col[0] += w
        dump(v_sb.ap().bitcast(F32), 16384)
        dump(ao_sb.ap().bitcast(F32), 1024)
        for G in range(4):
            dump(lq[G].ap().bitcast(F32), 1024)
        dump(p1_sb.ap().bitcast(F32), 256)
        dump(q2_sb.ap().bitcast(F32), 256)
        dump(ln2t.ap().bitcast(F32), 1024)
        dump(x2_sb.ap(), 2048)
        dump(yt_sb.ap(), 2048)
        dump(rb_sb.ap(), 512)
    return _realize(nc, sems, steps, dO)


def _realize(nc, sems, steps, dO):
    semguards = []
    for s in sems:
        g = nc.semaphore(s.name)
        s.h = g.__enter__()
        semguards.append(g)
    final_out = dO.n
    steps["sync"].append(lambda e: e.wait_ge(dO.h, final_out))
    blk = nc.Block()
    block = blk.__enter__()
    for eng_name, meth in (("sync", block.sync), ("scalar", block.scalar),
                           ("vector", block.vector), ("tensor", block.tensor),
                           ("gpsimd", block.gpsimd)):
        def body(e, lst=steps[eng_name]):
            for f in lst:
                f(e)
        meth(body)
    blk.__exit__(None, None, None)
    for g in semguards[::-1]:
        g.__exit__(None, None, None)
    return nc


_NC_CACHE = None


def kernel(L0_attn_x, L0_attn_W1_q, L0_attn_W2_q, L0_attn_W1_k, L0_attn_W2_k,
           L0_attn_W_v, L0_attn_b_v, L0_attn_W_out, L0_attn_b_out,
           L0_ffn_U1, L0_ffn_V1, L0_ffn_b1, L0_ffn_U2, L0_ffn_V2, L0_ffn_b2):
    global _NC_CACHE
    bf = ml_dtypes.bfloat16
    x = np.ascontiguousarray(np.asarray(L0_attn_x, dtype=np.float32))
    B = x.shape[0]

    perm = np.array([(j % 16) * 4 + j // 16 for j in range(64)])
    wq = np.ascontiguousarray(
        np.concatenate([np.asarray(L0_attn_W1_q)[perm],
                        np.asarray(L0_attn_W2_q)[perm]], axis=0).T
        .astype(np.float32))
    wk = np.ascontiguousarray(
        np.concatenate([np.asarray(L0_attn_W1_k)[perm],
                        np.asarray(L0_attn_W2_k)[perm]], axis=0).T
        .astype(np.float32))
    wv = np.ascontiguousarray(np.asarray(L0_attn_W_v, dtype=np.float32).T)
    wout = np.ascontiguousarray(np.asarray(L0_attn_W_out, dtype=np.float32).T)
    bv = np.ascontiguousarray(np.asarray(L0_attn_b_v, dtype=np.float32)[None, :])
    bout = np.ascontiguousarray(
        np.asarray(L0_attn_b_out, dtype=np.float32).reshape(8, 128).T)
    u1 = np.ascontiguousarray(np.asarray(L0_ffn_U1, dtype=np.float32))
    v1 = np.ascontiguousarray(np.asarray(L0_ffn_V1, dtype=np.float32))
    b1 = np.ascontiguousarray(
        np.asarray(L0_ffn_b1, dtype=np.float32).reshape(32, 128).T)
    u2 = np.ascontiguousarray(np.asarray(L0_ffn_U2, dtype=np.float32))
    v2 = np.ascontiguousarray(np.asarray(L0_ffn_V2, dtype=np.float32))
    b2 = np.ascontiguousarray(
        np.asarray(L0_ffn_b2, dtype=np.float32).reshape(8, 128).T)
    ident = np.eye(128, dtype=bf)

    L = np.triu(np.ones((128, 128), dtype=np.float32))
    onesm = np.ones((128, QT), dtype=np.float32)
    zerosm = np.zeros((128, QT), dtype=np.float32)
    tri_a = np.concatenate([L, np.ones((128, 128), dtype=np.float32)], axis=1)
    tri_b = np.concatenate([np.zeros((128, 128), dtype=np.float32), L], axis=1)

    def core_masks(par):
        mk = np.zeros((NSLOT, 4, 128, QT), dtype=np.float32)
        for si in range(NSLOT):
            n = NPROF[si]
            t = TILES[par][si]
            ct = 2 * t + 2
            for j in range(4):
                c = n - 4 + j
                if c < ct - 2:
                    mk[si, j] = onesm
                elif c == ct - 2:
                    mk[si, j] = tri_a
                elif c == ct - 1:
                    mk[si, j] = tri_b
                else:
                    mk[si, j] = zerosm
        return mk.astype(bf)

    masks = {p: core_masks(p) for p in (0, 1)}

    if _NC_CACHE is None:
        _NC_CACHE = build_nc()
    nc = _NC_CACHE

    in_maps = []
    for core in range(8):
        s, par = core // 2, core % 2
        xs = np.ascontiguousarray(x[s])
        xq = np.ascontiguousarray(
            np.concatenate([xs[QT * t:QT * (t + 1)] for t in TILES[par]],
                           axis=0))
        in_maps.append({
            "x": xs, "xq": xq, "masks": masks[par], "ident": ident,
            "wq": wq, "wk": wk, "wv": wv, "bv": bv, "wout": wout,
            "bout": bout, "u1": u1, "v1": v1, "b1": b1, "u2": u2, "v2": v2,
            "b2": b2,
        })

    res = run_bass_kernel_spmd(nc, in_maps, core_ids=list(range(8)))
    y = np.empty((B, T, D), dtype=np.float32)
    for core in range(8):
        s, par = core // 2, core % 2
        yc = res.results[core]["y"]
        for si, t in enumerate(TILES[par]):
            y[s, QT * t:QT * (t + 1)] = yc[QT * si:QT * (si + 1)]
    return y


# version 12
# speedup vs baseline: 1.1472x; 1.0553x over previous
import os as _os
import numpy as np
import ml_dtypes
import concourse.bass as bass
import concourse.mybir as mybir
from concourse.bass_utils import run_bass_kernel_spmd

F32 = mybir.dt.float32
BF16 = mybir.dt.bfloat16
AF = mybir.ActivationFunctionType
OP = mybir.AluOpType
AXX = mybir.AxisListType.X

D = 1024
T = 4096
QT = 256
NSLOT = 8
NPROF = [4, 8, 12, 16, 20, 24, 28, 32]
TILES = {0: [0, 3, 4, 7, 8, 11, 12, 15],
         1: [1, 2, 5, 6, 9, 10, 13, 14]}
SCALE = float(6.0 ** -0.5)
QPAIRS = [(0, 1), (0, 2), (0, 3), (1, 2), (1, 3), (2, 3)]
KPAIRS = [(2, 3), (3, 1), (1, 2), (0, 3), (2, 0), (0, 1)]
NT_K = 32
FF = 4096
RK = 256
ONE_TENSOR_SLOTS = 5


def build_nc(stage_limit=99, slot_limit=8):
    nc = bass.Bass("TRN2", target_bir_lowering=False,
                   detect_race_conditions=False,
                   dynamic_dma_scratch_size=1024)

    d_x = nc.dram_tensor("x", [T, D], F32, kind="ExternalInput")
    d_xq = nc.dram_tensor("xq", [NSLOT * QT, D], F32, kind="ExternalInput")
    d_masks = nc.dram_tensor("masks", [NSLOT, 4, 128, QT], BF16, kind="ExternalInput")
    d_ident = nc.dram_tensor("ident", [128, 128], BF16, kind="ExternalInput")
    d_wq = nc.dram_tensor("wq", [D, 128], F32, kind="ExternalInput")
    d_wk = nc.dram_tensor("wk", [D, 128], F32, kind="ExternalInput")
    d_wv = nc.dram_tensor("wv", [D, D], F32, kind="ExternalInput")
    d_bv = nc.dram_tensor("bv", [1, D], F32, kind="ExternalInput")
    d_wout = nc.dram_tensor("wout", [D, D], F32, kind="ExternalInput")
    d_bout = nc.dram_tensor("bout", [128, 8], F32, kind="ExternalInput")
    d_u1 = nc.dram_tensor("u1", [D, RK], F32, kind="ExternalInput")
    d_v1 = nc.dram_tensor("v1", [RK, FF], F32, kind="ExternalInput")
    d_b1 = nc.dram_tensor("b1", [128, 32], F32, kind="ExternalInput")
    d_u2 = nc.dram_tensor("u2", [FF, RK], F32, kind="ExternalInput")
    d_v2 = nc.dram_tensor("v2", [RK, D], F32, kind="ExternalInput")
    d_b2 = nc.dram_tensor("b2", [128, 8], F32, kind="ExternalInput")
    d_y = nc.dram_tensor("y", [NSLOT * QT, D], F32, kind="ExternalOutput")
    DBG = bool(_os.environ.get("K_DBG"))
    d_jlk = nc.dram_tensor("jlk_s", [4, 128, T], BF16,
                           kind="ExternalOutput" if DBG else "Internal")
    d_dbg = nc.dram_tensor("dbg", [128, 40960], F32, kind="ExternalOutput") if DBG else None
    d_wouts = nc.dram_tensor("wout_s", [8, 128, 1024], BF16, kind="Internal")
    d_u1s = nc.dram_tensor("u1_s", [2, 128, 1024], BF16, kind="Internal")
    d_v1s = nc.dram_tensor("v1_s", [4, 128, 2048], BF16, kind="Internal")
    d_u2s = nc.dram_tensor("u2_s", [2, 128, 4096], BF16, kind="Internal")
    d_v2s = nc.dram_tensor("v2_s", [1, 128, 2048], BF16, kind="Internal")

    steps = {k: [] for k in ("sync", "scalar", "vector", "tensor", "gpsimd")}
    floors = {k: {} for k in steps}

    class Sem:
        __slots__ = ("name", "h", "n")

        def __init__(self, name):
            self.name, self.h, self.n = name, None, 0

    sems = []

    def mksem(name):
        s = Sem(name)
        sems.append(s)
        return s

    dL = mksem("dL")
    dO = mksem("dO")
    dS = mksem("dS")
    cV = mksem("cV")
    cA = mksem("cA")
    cT = mksem("cT")
    cG = mksem("cG")
    cSC = mksem("cSC")
    cEX = mksem("cEX")
    cMK = mksem("cMK")
    cAV = mksem("cAV")

    def em(eng, f):
        steps[eng].append(f)

    def W(eng, sem, v):
        if v <= 0:
            return
        if floors[eng].get(sem.name, 0) >= v:
            return
        floors[eng][sem.name] = v
        em(eng, lambda e, s=sem, vv=v: e.wait_ge(s.h, vv))

    def I(eng, build, incs=(), waits=()):
        for (sem, v) in waits:
            W(eng, sem, v)
        for (sem, k) in incs:
            sem.n += k

        assert len(incs) <= 1, "one inc per instruction"

        def run(e, b=build, incs=tuple(incs)):
            inst = b(e)
            if incs:
                inst.then_inc(incs[0][0].h, incs[0][1])
        em(eng, run)

    def dma(qeng, qsem, out_ap, in_ap, waits=()):
        I(qeng, lambda e, o=out_ap, i=in_ap: e.dma_start(o, i),
          incs=[(qsem, 16)], waits=waits)

    guards = []

    def sb(name, shape, dtype):
        g = nc.sbuf_tensor(name, shape, dtype)
        t = g.__enter__()
        guards.append(g)
        return t

    def psum(name, shape, dtype):
        g = nc.psum_tensor(name, shape, dtype)
        t = g.__enter__()
        guards.append(g)
        return t

    ps_s0 = psum("ps_s0", [128, 1024], F32)
    ps_s1 = psum("ps_s1", [128, 1024], F32)
    ps_nAB = psum("ps_nAB", [128, 512], F32)
    ps_nCD = psum("ps_nCD", [128, 512], F32)
    ps_den = psum("ps_den", [128, 512], F32)
    ps_u = psum("ps_u", [128, 512], F32)

    v_sb = sb("v_sb", [128, NT_K * 1024], BF16)
    lq = [sb(f"lq{G}", [128, NSLOT * QT], BF16) for G in range(4)]
    mbuf = [sb(f"mbuf{i}", [128, 4 * QT], BF16) for i in range(2)]
    p_sb = [sb(f"p{j}", [128, 1024], BF16) for j in range(8)]
    jbuf = [sb(f"jbuf{i}", [128, 4096], BF16) for i in range(2)]
    wbuf = [sb(f"wbuf{i}", [128, 4096], BF16) for i in range(2)]
    wv_sb = sb("wv_sb", [128, 8 * 1024], BF16)
    wq_sb = sb("wq_sb", [128, 8 * 128], BF16)
    wk_sb = sb("wk_sb", [128, 8 * 128], BF16)
    ao_sb = sb("ao_sb", [128, 8 * QT], BF16)
    h_sb = sb("h_sb", [128, 32 * QT], BF16)
    xin = sb("xin", [128, D], F32)
    xnb = sb("xnb", [128, D], BF16)
    lnxt = sb("lnxt", [128, 8 * 128], BF16)
    ln2t = sb("ln2t", [128, 8 * QT], BF16)
    tmbf = [sb(f"tmbf{i}", [128, 512], BF16) for i in range(2)]
    jstage = sb("jstage", [128, 512], BF16)
    e_t1 = sb("e_t1", [128, 128], F32)
    e_t2 = sb("e_t2", [128, 128], F32)
    e_sq = sb("e_sq", [128, 96], F32)
    e_ss = sb("e_ss", [128, 16], F32)
    e_rs = sb("e_rs", [128, 16], F32)
    k12 = sb("k12", [128, 128], F32)
    x2_sb = sb("x2_sb", [128, 2 * D], F32)
    yt_sb = sb("yt_sb", [128, 2 * D], F32)
    ytp = sb("ytp", [128, 256], F32)
    p1_sb = sb("p1_sb", [128, 2 * QT], BF16)
    q2_sb = sb("q2_sb", [128, 2 * QT], BF16)
    stage = [h_sb.ap().bitcast(F32)[:, 0:1024],
             h_sb.ap().bitcast(F32)[:, 1024:2048]]
    bstage = [h_sb.ap()[:, 4096:5120], h_sb.ap()[:, 5120:6144]]
    xqin = sb("xqin", [128, D], F32)
    bn_sb = sb("bn_sb", [128, 12], F32)
    mv_sb = sb("mv_sb", [128, 2], F32)
    rstd_sb = sb("rstd_sb", [128, 1], F32)
    eps_sb = sb("eps_sb", [128, 1], F32)
    eps24_sb = sb("eps24_sb", [128, 1], F32)
    ident_sb = sb("ident_sb", [128, 128], BF16)
    identf_sb = sb("identf_sb", [128, 128], F32)
    ones_bf = sb("ones_bf", [128, 1], BF16)
    onesr = sb("onesr", [128, 128], BF16)
    bv_sb = sb("bv_sb", [1, D], BF16)
    b1_sb = sb("b1_sb", [128, 32], F32)
    bout_sb = sb("bout_sb", [128, 8], F32)
    b2_sb = sb("b2_sb", [128, 8], F32)
    spc_sb = sb("spc", [128, 4], F32)
    rb_sb = sb("rb_sb", [128, 512], F32)
    rd_sb = sb("rd_sb", [128, 1024], BF16)

    wv_view = wv_sb.ap().rearrange("p (c n) -> p c n", c=8)
    s1bf = ps_s1.ap().bitcast(BF16)
    ubf = ps_u.ap().bitcast(BF16)

    def mm(out, lhsT, rhs, start=True, stop=True, tp=None, incs=(), waits=()):
        I("tensor",
          lambda e, o=out, l=lhsT, r=rhs, st=start, sp=stop, tpp=tp:
          e.matmul(o, l, r, start=st, stop=sp, tile_position=tpp,
                   skip_group_check=True),
          incs=incs, waits=waits)

    def tr(out, in_, incs=(), waits=()):
        I("tensor",
          lambda e, o=out, i=in_, idn=(identf_sb if in_.dtype == F32 else ident_sb):
          e.transpose(o, i, idn[:]),
          incs=incs, waits=waits)

    def act(out, in_, func, bias=0.0, scale=1.0, incs=(), waits=()):
        I("scalar",
          lambda e, o=out, i=in_, f=func, b=bias, s=scale:
          e.activation(o, i, f, bias=b, scale=s),
          incs=incs, waits=waits)

    guard = {"xin": 0, "xnb": 0, "tmbf0": 0, "tmbf1": 0, "ao": 0}

    def vfence():
        W("vector", cV, cV.n)

    def ln_tile(src_ap, dst_ap, dst_guard_T=0):
        I("vector", lambda e, s=src_ap: e.bn_stats(bn_sb[:, 0:6], s[:, 0:512]))
        I("vector", lambda e, s=src_ap: e.bn_stats(bn_sb[:, 6:12], s[:, 512:1024]),
          incs=[(cV, 1)])
        vfence()
        I("vector", lambda e: e.bn_aggr(mv_sb[:], bn_sb[:]), incs=[(cV, 1)])
        act(rstd_sb[:], mv_sb[:, 1:2], AF.Ln, bias=eps_sb[:], waits=[(cV, cV.n)])
        act(rstd_sb[:], rstd_sb[:], AF.Exp, scale=-0.5, incs=[(cA, 1)])
        I("vector",
          lambda e, s=src_ap, d=dst_ap: e.tensor_scalar(
              d, s, mv_sb[:, 0:1], rstd_sb[:], op0=OP.subtract, op1=OP.mult),
          incs=[(cV, 1)], waits=[(cA, cA.n), (cT, dst_guard_T)])

    def exterior(src_ap, pairs, dst_ap, dst_guard_T=0):
        def v1c(c):
            return src_ap[:, c * 16:c * 16 + 16].rearrange(
                "p (g a) -> p g a", g=4).unsqueeze(3)

        def v2c(c):
            return src_ap[:, 64 + c * 16:64 + c * 16 + 16].rearrange(
                "p (g a) -> p g a", g=4).unsqueeze(3)

        t1v = e_t1.ap().rearrange("p (g a x) -> p g a x", g=4, a=4, x=8)
        t2v = e_t2.ap().rearrange("p (g a x) -> p g a x", g=4, a=4, x=8)
        vfence()
        for p, (a, b) in enumerate(pairs):
            I("vector", lambda e, o=t1v[:, :, :, p:p + 1], x=v1c(a), y=v2c(b):
              e.tensor_tensor(o, x, y, op=OP.mult))
            I("vector", lambda e, o=t2v[:, :, :, p:p + 1], x=v1c(b), y=v2c(a):
              e.tensor_tensor(o, x, y, op=OP.mult))
        Lsp = t1v[:, :, :, 0:6]
        I("vector", lambda e, L=Lsp: e.tensor_tensor(L, L, t2v[:, :, :, 0:6],
                                                     op=OP.subtract))
        sqv = e_sq.ap().rearrange("p (g a x) -> p g a x", g=4, a=4, x=6)
        I("vector", lambda e, L=Lsp, s=sqv: e.tensor_tensor(s, L, L, op=OP.mult))
        I("vector", lambda e, s=sqv: e.reduce_sum(
            e_ss.ap().rearrange("p (g a) -> p g a", g=4).unsqueeze(3),
            s, axis=AXX), incs=[(cV, 1)])
        act(e_rs[:], e_ss[:], AF.Ln, bias=eps24_sb[:], waits=[(cV, cV.n)])
        act(e_rs[:], e_rs[:], AF.Exp, scale=-0.5, incs=[(cA, 1)])
        rsb = e_rs.ap().rearrange("p (g a) -> p g a", g=4).unsqueeze(3) \
            .broadcast_to([128, 4, 4, 6])
        I("vector", lambda e, d=dst_ap, L=Lsp, r=rsb: e.tensor_tensor(
            d.rearrange("p (g a x) -> p g a x", g=4, a=4)[:, :, :, 0:6],
            L, r, op=OP.mult),
          incs=[(cV, 1)], waits=[(cA, cA.n), (cT, dst_guard_T)])

    castv = [0, 0]
    bst_free = [0, 0]

    def spill(piece_idx, in_ap, out_ap, to_sbuf=None):
        b = piece_idx % 2
        dma("scalar", dS, stage[b][:, 0:in_ap.size() // 128], in_ap,
            waits=[(cV, castv[b])])
        dv = dS.n
        w = out_ap.size() // 128 if out_ap is not None else to_sbuf.size() // 128
        if to_sbuf is not None:
            I("vector", lambda e, o=to_sbuf, w=w: e.tensor_copy(
                o, stage[b][:, 0:w]), incs=[(cV, 1)], waits=[(dS, dv)])
            castv[b] = cV.n
        else:
            I("vector", lambda e, w=w: e.tensor_copy(
                bstage[b][:, 0:w], stage[b][:, 0:w]),
              incs=[(cV, 1)], waits=[(dS, dv)])
            castv[b] = cV.n
            dma("scalar", dS, out_ap, bstage[b].rearrange(
                "p (a n) -> p a n", a=out_ap.shape[-2] if out_ap.ndim == 3 else 1)
                if False else bstage[b][:, 0:w],
                waits=[(cV, cV.n)])

    def startup():
        I("vector", lambda e: e.memset(eps_sb[:], 1e-5))
        I("vector", lambda e: e.memset(eps24_sb[:], 1e-24))
        I("vector", lambda e: e.memset(ones_bf[:], 1.0))
        I("vector", lambda e: e.memset(onesr[:], 1.0))
        I("vector", lambda e: e.memset(e_t1[:], 0.0))
        I("vector", lambda e: e.memset(e_t2[:], 0.0))
        I("vector", lambda e: e.memset(tmbf[0][:], 0.0))
        I("vector", lambda e: e.memset(tmbf[1][:], 0.0), incs=[(cV, 1)])
        dma("sync", dL, ident_sb[:], d_ident.ap())
        I("vector", lambda e: e.tensor_copy(identf_sb[:], ident_sb[:]),
          incs=[(cV, 1)], waits=[(dL, dL.n)])

        dma("sync", dL, b1_sb[:], d_b1.ap())
        dma("sync", dL, bout_sb[:], d_bout.ap())
        dma("sync", dL, b2_sb[:], d_b2.ap())
        pi = 0
        spill(pi, d_wq.ap().rearrange("(r p) n -> p r n", p=128), None,
              to_sbuf=wq_sb[:]); pi += 1
        spill(pi, d_wk.ap().rearrange("(r p) n -> p r n", p=128), None,
              to_sbuf=wk_sb[:]); pi += 1
        dma("scalar", dS, stage[0][0:1, :], d_bv.ap(), waits=[(cV, castv[0])])
        I("vector", lambda e: e.tensor_copy(bv_sb[:], stage[0][0:1, :]),
          incs=[(cV, 1)], waits=[(dS, dS.n)])
        castv[0] = cV.n
        for r in range(8):
            spill(pi, d_wv.ap()[r * 128:(r + 1) * 128, :], None,
                  to_sbuf=wv_sb[:, r * 1024:(r + 1) * 1024]); pi += 1
        for j in range(8):
            spill(pi, d_wout.ap().rearrange("(r p) n -> p r n", p=128)
                  [:, :, j * 128:(j + 1) * 128], d_wouts.ap()[j]); pi += 1
        for m in range(2):
            spill(pi, d_u1.ap().rearrange("(r p) n -> p r n", p=128)
                  [:, :, m * 128:(m + 1) * 128], d_u1s.ap()[m]); pi += 1
        for q in range(4):
            for rh in range(2):
                spill(pi, d_v1.ap()[rh * 128:(rh + 1) * 128,
                                    q * 1024:(q + 1) * 1024],
                      d_v1s.ap()[q].rearrange("p (c x n) -> p c x n", c=8, x=2)
                      [:, :, rh, :]); pi += 1
        for m in range(2):
            for cg in range(4):
                spill(pi, d_u2.ap().rearrange("(c p) n -> p c n", p=128)
                      [:, cg * 8:(cg + 1) * 8, m * 128:(m + 1) * 128],
                      d_u2s.ap()[m][:, cg * 1024:(cg + 1) * 1024]); pi += 1
        for m in range(2):
            spill(pi, d_v2.ap()[m * 128:(m + 1) * 128, :],
                  d_v2s.ap()[0].rearrange("p (j x n) -> p j x n", j=8, x=2)
                  [:, :, m, :]); pi += 1

    ph1V = []
    ph1J = []

    def ph1_tile(t):
        dma("sync", dL, xin[:], d_x.ap()[t * 128:(t + 1) * 128, :],
            waits=[(cV, guard["xin"])])
        W("vector", dL, dL.n)
        ln_tile(xin.ap(), xnb.ap(), dst_guard_T=guard["xnb"])
        guard["xin"] = cV.n
        lnv = cV.n
        for r in range(8):
            tr(s1bf[:, r * 128:(r + 1) * 128], xnb[:, r * 128:(r + 1) * 128],
               waits=([(cV, lnv), (cA, cA.n)] if r == 0 else ()),
               incs=([(cT, 1)] if r == 7 else ()))
        guard["xnb"] = cT.n
        I("vector", lambda e: e.tensor_copy(lnxt[:], s1bf[:, 0:1024]),
          incs=[(cV, 1)], waits=[(cT, cT.n)])
        lnxtv = cV.n
        for hh in range(2):
            dst = ps_s1[:, hh * 512:(hh + 1) * 512]
            for r in range(8):
                mm(dst, lnxt[:, r * 128:(r + 1) * 128],
                   wv_view[:, r, hh * 512:(hh + 1) * 512],
                   start=(r == 0), stop=False,
                   waits=([(cV, lnxtv), (cA, cA.n)] if r == 0 else ()))
            mm(dst, onesr[0:1, :], bv_sb[0:1, hh * 512:(hh + 1) * 512],
               start=False, stop=True, incs=([(cT, 1)] if hh == 1 else ()))
        act(v_sb[:, t * 1024:(t + 1) * 1024], ps_s1[:], AF.Copy,
            incs=[(cA, 1)], waits=[(cT, cT.n)])
        ph1V.append(cA.n)
        for r in range(8):
            mm(ps_s1[:, 0:128], lnxt[:, r * 128:(r + 1) * 128],
               wk_sb[:, r * 128:(r + 1) * 128],
               start=(r == 0), stop=(r == 7),
               waits=([(cA, cA.n)] if r == 0 else ()),
               incs=([(cT, 1)] if r == 7 else ()))
        I("vector", lambda e: e.tensor_copy(k12[:], ps_s1[:, 0:128]),
          incs=[(cV, 1)], waits=[(cT, cT.n)])
        buf = t % 2
        exterior(k12.ap(), KPAIRS, tmbf[buf].ap(),
                 dst_guard_T=guard[f"tmbf{buf}"])
        extv = cV.n
        for G in range(4):
            tr(s1bf[:, 1024 + G * 128:1024 + (G + 1) * 128],
               tmbf[buf][:, G * 128:(G + 1) * 128],
               waits=([(cV, extv), (cA, cA.n)] if G == 0 else ()),
               incs=([(cT, 1)] if G == 3 else ()))
        guard[f"tmbf{buf}"] = cT.n
        I("vector", lambda e: e.tensor_copy(jstage[:], s1bf[:, 1024:1536]),
          incs=[(cV, 1)], waits=[(cT, cT.n), (dO, (ph1J[-1] if ph1J else 0))])
        dma("sync", dO,
            d_jlk.ap()[:, :, t * 128:(t + 1) * 128].transpose([1, 0, 2]),
            jstage.ap().rearrange("p (g n) -> p g n", g=4),
            waits=[(cV, cV.n)])
        ph1J.append(dO.n)

    ph1Q = [0] * NSLOT

    def ph1b_slot(si):
        for tt in range(2):
            tq = si * 2 + tt
            dma("sync", dL, xin[:], d_xq.ap()[tq * 128:(tq + 1) * 128, :],
                waits=[(cV, guard["xin"])])
            W("vector", dL, dL.n)
            ln_tile(xin.ap(), xnb.ap(), dst_guard_T=guard["xnb"])
            guard["xin"] = cV.n
            lnv = cV.n
            for r in range(8):
                tr(s1bf[:, r * 128:(r + 1) * 128], xnb[:, r * 128:(r + 1) * 128],
                   waits=([(cV, lnv), (cA, cA.n)] if r == 0 else ()),
                   incs=([(cT, 1)] if r == 7 else ()))
            guard["xnb"] = cT.n
            I("vector", lambda e: e.tensor_copy(lnxt[:], s1bf[:, 0:1024]),
              incs=[(cV, 1)], waits=[(cT, cT.n)])
            lnxv = cV.n
            for r in range(8):
                mm(ps_s1[:, 0:128], lnxt[:, r * 128:(r + 1) * 128],
                   wq_sb[:, r * 128:(r + 1) * 128],
                   start=(r == 0), stop=(r == 7),
                   waits=([(cV, lnxv), (cA, cA.n)] if r == 0 else ()),
                   incs=([(cT, 1)] if r == 7 else ()))
            I("vector", lambda e: e.tensor_copy(k12[:], ps_s1[:, 0:128]),
              incs=[(cV, 1)], waits=[(cT, cT.n)])
            buf = tt
            exterior(k12.ap(), QPAIRS, tmbf[buf].ap(),
                     dst_guard_T=guard[f"tmbf{buf}"])
            extv = cV.n
            for G in range(4):
                tr(s1bf[:, 1024 + G * 128:1024 + (G + 1) * 128],
                   tmbf[buf][:, G * 128:(G + 1) * 128],
                   waits=([(cV, extv), (cA, cA.n)] if G == 0 else ()),
                   incs=([(cT, 1)] if G == 3 else ()))
            guard[f"tmbf{buf}"] = cT.n
            for G in range(4):
                I("vector", lambda e, G=G, si=si, tt=tt: e.tensor_copy(
                    lq[G][:, si * QT + tt * 128: si * QT + (tt + 1) * 128],
                    s1bf[:, 1024 + G * 128:1024 + (G + 1) * 128]),
                  incs=([(cV, 1)] if G == 3 else ()), waits=[(cT, cT.n)])
        ph1Q[si] = cV.n

    ggi = [0]
    quad_no = [0]
    s_free = {0: (0, 0), 1: (0, 0)}
    jbuf_free = [0, 0]
    jd = {}
    exp_after = {}
    mask_after = {}

    mload = [0] * NSLOT

    def mask_load(si):
        dma("sync", dL, mbuf[si % 2].ap().rearrange("p (j q) -> p j q", j=4),
            d_masks.ap()[si].transpose([1, 0, 2]),
            waits=[(cMK, mbuf_free[si % 2])])
        mload[si] = dL.n

    mbuf_free = [0, 0]

    def jlk_prefetch(si, G):
        n = NPROF[si]
        b = G % 2
        dma("sync", dL, jbuf[b][:, 0:n * 128], d_jlk.ap()[G][:, 0:n * 128],
            waits=[(cSC, jbuf_free[b]), (dO, ph1J[n - 1])])
        jd[(si, G)] = dL.n

    def emit_av(si, G, my_quad, gj, h_ggi, chunks, masked, ngrp):
        slabs = [p_sb[(h_ggi % 2) * 4 + a] for a in range(4)]
        first = gj == 0
        last = gj == ngrp - 1
        for ci, c in enumerate(chunks):
            w = []
            if ci == 0:
                for a in range(4):
                    if masked:
                        w.append((cMK, mask_after[(h_ggi, a)]))
                    else:
                        w.append((cEX, exp_after[(h_ggi, a)]))
                w.append((cA, ph1V[max(chunks)]))
                if first and my_quad >= 1:
                    w.append((cV, ev_vals[my_quad - 1]))
            st = first and ci == 0
            sp = last and ci == 3
            for a in range(4):
                nb = ps_nAB if a < 2 else ps_nCD
                vcol = c * 1024 + (4 * G + a) * 64
                base = 64 * (a % 2)
                mm(nb[base:base + 64, 0:256],
                   v_sb[:, vcol:vcol + 64],
                   slabs[a][:, ci * 256:(ci + 1) * 256],
                   start=st, stop=sp, tp=(0, base),
                   waits=(w if a == 0 else ()))
            dendst = [ps_nAB[0:1, 256:512], ps_den[0:1, 0:256],
                      ps_nCD[0:1, 256:512], ps_den[0:1, 256:512]]
            for a in range(4):
                mm(dendst[a], ones_bf[:],
                   slabs[a][:, ci * 256:(ci + 1) * 256],
                   start=(st if a == 1 else False), stop=sp, tp=(0, 0),
                   incs=([(cAV, 1)] if (a == 3 and ci == 3) else ()))

    def attn_quad(si, G):
        n = NPROF[si]
        ngrp = n // 4
        two_t = si >= ONE_TENSOR_SLOTS
        grp = [list(range(n - 4, n))] + \
              [list(range(4 * g, 4 * g + 4)) for g in range(ngrp - 1)]
        qv = ph1Q[si]
        my_quad = quad_no[0]
        quad_no[0] += 1
        jdv = jd[(si, G)]
        pend = []
        for gi, chunks in enumerate(grp):
            masked = gi == 0
            g_ggi = ggi[0]
            ggi[0] += 1
            slabs = [p_sb[(g_ggi % 2) * 4 + a] for a in range(4)]
            ring_wait = max(0, g_ggi - 1)
            sc_done = {}

            def sc_head(a, st):
                stv = ps_s0 if st == 0 else ps_s1
                for ci, c in enumerate(chunks):
                    w = ()
                    if ci == 0:
                        w = [(dL, jdv), (cV, qv), (cEX, s_free[st][0])]
                    elif ci == 2:
                        w = [(cEX, s_free[st][1])]
                    mm(stv[:, ci * 256:(ci + 1) * 256],
                       jbuf[G % 2][32 * a:32 * a + 6, c * 128:(c + 1) * 128],
                       lq[G][32 * a:32 * a + 6, si * QT:(si + 1) * QT],
                       tp=(32 * a, 0),
                       waits=w,
                       incs=([(cSC, 1)] if ci in (1, 3) else ()))
                sc_done[a] = cSC.n

            def exp_head(a, st):
                stv = ps_s0 if st == 0 else ps_s1
                act(slabs[a][:, 0:512], stv[:, 0:512], AF.Exp, scale=SCALE,
                    waits=[(cSC, sc_done[a] - 1), (cAV, ring_wait)],
                    incs=[(cEX, 1)])
                h0 = cEX.n
                act(slabs[a][:, 512:1024], stv[:, 512:1024], AF.Exp,
                    scale=SCALE, waits=[(cSC, sc_done[a])],
                    incs=[(cEX, 1)])
                exp_after[(g_ggi, a)] = cEX.n
                s_free[st] = (h0, cEX.n)

            if two_t:
                sc_head(0, 0); sc_head(2, 1)
                exp_head(0, 0); exp_head(2, 1)
                sc_head(1, 0); sc_head(3, 1)
                exp_head(1, 0); exp_head(3, 1)
            else:
                for a in range(4):
                    sc_head(a, 0)
                    exp_head(a, 0)
            if masked:
                for a in range(4):
                    I("vector", lambda e, sl=slabs[a], si=si: e.tensor_tensor(
                        sl[:], sl[:], mbuf[si % 2][:], op=OP.mult),
                      waits=[(cEX, exp_after[(g_ggi, a)]), (dL, mload[si])],
                      incs=[(cMK, 1)])
                    mask_after[(g_ggi, a)] = cMK.n
            pend.append((gi, g_ggi, chunks, masked))
            if gi >= 1:
                gj, h, chs, msk = pend.pop(0)
                emit_av(si, G, my_quad, gj, h, chs, msk, ngrp)
        for (gj, h, chs, msk) in pend:
            emit_av(si, G, my_quad, gj, h, chs, msk, ngrp)

    def attn_tail(si, G):
        avv = ggi[0]
        densrc = [ps_nAB[0:1, 256:512], ps_den[0:1, 0:256],
                  ps_nCD[0:1, 256:512], ps_den[0:1, 256:512]]
        def _recip(e, a, s):
            with nc.allow_low_precision(reason="recip cast to bf16 for matmul"
                                        " rhs; same precision as prior"
                                        " fp32-recip + bf16-copy path"):
                return e.reciprocal(rd_sb[0:1, 256 * a:256 * (a + 1)], s)
        for a in range(4):
            I("vector", lambda e, a=a, s=densrc[a]: _recip(e, a, s),
              waits=([(cAV, avv)] if a == 0 else ()),
              incs=([(cV, 1)] if a == 3 else ()))
        rdv = cV.n
        bc = [(0, ps_u[0:32, 0:256], (0, 0)), (0, ps_u[32:64, 0:256], (0, 32)),
              (1, ps_u[64:96, 0:256], (0, 64)), (1, ps_u[96:128, 0:256], (0, 96)),
              (2, ps_u[0:32, 256:512], (0, 0)), (2, ps_u[32:64, 256:512], (0, 32)),
              (3, ps_u[64:96, 256:512], (0, 64)), (3, ps_u[96:128, 256:512], (0, 96))]
        for i, (a, dst, tpp) in enumerate(bc):
            mm(dst, onesr[0:1, 0:32], rd_sb[0:1, 256 * a:256 * (a + 1)],
               tp=tpp,
               waits=([(cV, max(rdv, b7V[0])), (cA, b7A[0])] if i == 0 else ()),
               incs=([(cT, 1)] if i == 7 else ()))
        bcv = cT.n
        I("vector", lambda e: e.tensor_copy(rb_sb[:], ps_u[:]),
          waits=[(cT, bcv)], incs=[(cV, 1)])
        b7V[0] = cV.n
        for a in range(4):
            h = 4 * G + a
            nsrc = ps_nAB if a < 2 else ps_nCD
            base = 64 * (a % 2)
            ucol = 0 if a < 2 else 256
            I("vector", lambda e, h=h, nsrc=nsrc, base=base, ucol=ucol:
              e.tensor_tensor(
                  ao_sb[64 * (h % 2):64 * (h % 2) + 64,
                        (h // 2) * QT:(h // 2 + 1) * QT],
                  nsrc[base:base + 64, 0:256],
                  rb_sb[base:base + 64, ucol:ucol + 256], op=OP.mult),
              waits=[(cT, guard["ao"])],
              incs=([(cV, 1)] if a == 3 else ()))
        ev_vals.append(cV.n)

    wbuf_free = [0, 0]
    ev_vals = []
    b7A = [0]
    b7V = [0]
    wpc = [0]
    h_guard = [0]
    h_acts = {}
    out_dma = [0]
    spill_done = [0]

    def wpiece(dram_ap, ncols):
        b = wpc[0] % 2
        wpc[0] += 1
        dma("scalar", dS, wbuf[b][:, 0:ncols], dram_ap,
            waits=[(cT, wbuf_free[b]), (dS, spill_done[0])])
        return b, dS.n

    def piece_P0(si):
        for j in range(8):
            b, dv = wpiece(d_wouts.ap()[j], 1024)
            for rr in range(8):
                mm(ps_u[:, 0:256], wbuf[b][:, rr * 128:(rr + 1) * 128],
                   ao_sb[:, rr * QT:(rr + 1) * QT],
                   start=(rr == 0), stop=(rr == 7),
                   waits=([(dS, dv), (cV, ev_vals[4 * (si + 1) - 1]),
                           (cA, b7A[0]), (cV, b7V[0])]
                          if rr == 0 else ()),
                   incs=([(cT, 1)] if rr == 7 else ()))
            wbuf_free[b] = cT.n
            act(ytp[:], ps_u[:, 0:256], AF.Identity, bias=bout_sb[:, j:j + 1],
                incs=[(cA, 1)], waits=[(cT, cT.n)])
            b7A[0] = cA.n
            for tt in range(2):
                tr(ps_u[:, tt * 128:(tt + 1) * 128], ytp[:, tt * 128:(tt + 1) * 128],
                   waits=([(cA, cA.n)] if tt == 0 else ()),
                   incs=([(cT, 1)] if tt == 1 else ()))
            I("vector", lambda e, j=j: e.tensor_copy(
                yt_sb.ap().rearrange("p (t n) -> p t n", t=2)
                [:, :, j * 128:(j + 1) * 128],
                ps_u.ap()[:, 0:256].rearrange("p (t n) -> p t n", t=2)),
              incs=[(cV, 1)], waits=[(cT, cT.n)])
            b7V[0] = cV.n

    def piece_P1(si):
        for tt in range(2):
            dma("sync", dL, xqin[:], d_xq.ap()[(si * 2 + tt) * 128:
                                               (si * 2 + tt + 1) * 128, :],
                waits=[(cG, cG.n)])
            ldv = dL.n
            yv = yt_sb.ap()[:, tt * 1024:(tt + 1) * 1024]
            ln_tile(yv, yv)
            lnv = cV.n
            I("gpsimd", lambda e, tt=tt, yv=yv: e.tensor_add(
                x2_sb[:, tt * 1024:(tt + 1) * 1024], xqin[:], yv),
              waits=[(cV, lnv), (dL, ldv), (dO, out_dma[0])], incs=[(cG, 1)])
            W("vector", cG, cG.n)
            ln_tile(x2_sb.ap()[:, tt * 1024:(tt + 1) * 1024], xnb.ap(),
                    dst_guard_T=guard["xnb"])
            lnv2 = cV.n
            for half in range(2):
                for r in range(4):
                    tr(ubf[:, half * 512 + r * 128: half * 512 + (r + 1) * 128],
                       xnb[:, half * 512 + r * 128: half * 512 + (r + 1) * 128],
                       waits=([(cV, max(lnv2, b7V[0])), (cA, b7A[0])]
                              if r == 0 else ()),
                       incs=([(cT, 1)] if r == 3 else ()))
                if half == 1:
                    guard["xnb"] = cT.n
                I("vector", lambda e, tt=tt, half=half: e.tensor_copy(
                    ln2t.ap().rearrange("p (r q) -> p r q", r=8)
                    [:, half * 4:(half + 1) * 4, tt * 128:(tt + 1) * 128],
                    ubf.rearrange("p (r q) -> p r q", r=8)
                    [:, half * 4:(half + 1) * 4, :]),
                  incs=[(cV, 1)], waits=[(cT, cT.n)])
                b7V[0] = cV.n

    def piece_P2(si):
        l2v = cV.n
        for m in range(2):
            b, dv = wpiece(d_u1s.ap()[m], 1024)
            for rr in range(8):
                mm(ps_u[:, 0:256], wbuf[b][:, rr * 128:(rr + 1) * 128],
                   ln2t[:, rr * QT:(rr + 1) * QT],
                   start=(rr == 0), stop=(rr == 7),
                   waits=([(dS, dv), (cV, max(l2v, b7V[0])), (cA, b7A[0])]
                          if rr == 0 else ()),
                   incs=([(cT, 1)] if rr == 7 else ()))
            wbuf_free[b] = cT.n
            act(p1_sb[:, m * QT:(m + 1) * QT], ps_u[:, 0:256], AF.Copy,
                incs=[(cA, 1)], waits=[(cT, cT.n)])
            b7A[0] = cA.n
        p1v = cA.n
        for q in range(4):
            b, dv = wpiece(d_v1s.ap()[q], 2048)
            for cc8 in range(8):
                cc = q * 8 + cc8
                for rh in range(2):
                    mm(ps_u[:, 0:256],
                       wbuf[b][:, cc8 * 256 + rh * 128: cc8 * 256 + (rh + 1) * 128],
                       p1_sb[:, rh * QT:(rh + 1) * QT],
                       start=(rh == 0), stop=(rh == 1),
                       waits=([(dS, dv), (cA, max(p1v, b7A[0])), (cV, b7V[0])]
                              if rh == 0 else ()),
                       incs=([(cT, 1)] if rh == 1 else ()))
                act(h_sb[:, cc * QT:(cc + 1) * QT], ps_u[:, 0:256],
                    AF.Gelu_apprx_tanh, bias=b1_sb[:, cc:cc + 1],
                    incs=[(cA, 1)],
                    waits=[(cT, cT.n), (cT, h_guard[0])])
                b7A[0] = cA.n
                h_acts[cc] = cA.n
            wbuf_free[b] = cT.n

    def piece_P3(si):
        for m in range(2):
            b, dv = wpiece(d_u2s.ap()[m], 4096)
            for cc in range(32):
                w = [(cA, h_acts[cc])] if m == 0 else []
                if cc == 0:
                    w += [(dS, dv), (cA, b7A[0]), (cV, b7V[0])]
                mm(ps_u[:, 0:256], wbuf[b][:, cc * 128:(cc + 1) * 128],
                   h_sb[:, cc * QT:(cc + 1) * QT],
                   start=(cc == 0), stop=(cc == 31),
                   waits=w, incs=([(cT, 1)] if cc == 31 else ()))
            wbuf_free[b] = cT.n
            act(q2_sb[:, m * QT:(m + 1) * QT], ps_u[:, 0:256], AF.Copy,
                incs=[(cA, 1)], waits=[(cT, cT.n)])
            b7A[0] = cA.n
        h_guard[0] = cT.n
        q2v = cA.n
        b, dv = wpiece(d_v2s.ap()[0], 2048)
        for j in range(8):
            for m in range(2):
                mm(ps_u[:, 0:256],
                   wbuf[b][:, j * 256 + m * 128: j * 256 + (m + 1) * 128],
                   q2_sb[:, m * QT:(m + 1) * QT],
                   start=(m == 0), stop=(m == 1),
                   waits=([(dS, dv), (cA, max(q2v, b7A[0])), (cV, b7V[0])]
                          if m == 0 else ()),
                   incs=([(cT, 1)] if m == 1 else ()))
            act(ytp[:], ps_u[:, 0:256], AF.Identity, bias=b2_sb[:, j:j + 1],
                incs=[(cA, 1)], waits=[(cT, cT.n)])
            b7A[0] = cA.n
            for tt in range(2):
                tr(ps_u[:, tt * 128:(tt + 1) * 128],
                   ytp[:, tt * 128:(tt + 1) * 128],
                   waits=([(cA, cA.n)] if tt == 0 else ()),
                   incs=([(cT, 1)] if tt == 1 else ()))
            I("vector", lambda e, j=j: e.tensor_copy(
                yt_sb.ap().rearrange("p (t n) -> p t n", t=2)
                [:, :, j * 128:(j + 1) * 128],
                ps_u.ap()[:, 0:256].rearrange("p (t n) -> p t n", t=2)),
              incs=[(cV, 1)], waits=[(cT, cT.n)])
            b7V[0] = cV.n
        wbuf_free[b] = cT.n
        for tt in range(2):
            yv = yt_sb.ap()[:, tt * 1024:(tt + 1) * 1024]
            ln_tile(yv, yv)
            lnv = cV.n
            I("gpsimd", lambda e, tt=tt, yv=yv: e.tensor_add(
                x2_sb[:, tt * 1024:(tt + 1) * 1024],
                x2_sb[:, tt * 1024:(tt + 1) * 1024], yv),
              waits=[(cV, lnv)], incs=[(cG, 1)])
            dma("sync", dO,
                d_y.ap()[(si * 2 + tt) * 128:(si * 2 + tt + 1) * 128, :],
                x2_sb[:, tt * 1024:(tt + 1) * 1024], waits=[(cG, cG.n)])
        out_dma[0] = dO.n

    pieces = [piece_P0, piece_P1, piece_P2, piece_P3]

    startup()
    spill_done[0] = dS.n
    W("tensor", dL, dL.n)
    if stage_limit < 1:
        return _realize(nc, sems, steps, dO)
    W("tensor", cV, cV.n)
    W("scalar", dL, dL.n)
    W("vector", dL, dL.n)
    W("gpsimd", dL, dL.n)
    ph1b_slot(0)
    ph1b_slot(1)
    kt = [0]
    if stage_limit < 2:
        return _realize(nc, sems, steps, dO)

    def emit_ph1_until(ntiles):
        while kt[0] < ntiles:
            ph1_tile(kt[0])
            kt[0] += 1

    if stage_limit < 3:
        for tq in range(2, 8):
            ph1b_slot(tq)
        emit_ph1_until(NT_K)
        if DBG:
            W("sync", cV, cV.n)
            W("sync", cA, cA.n)
            col = [0]

            def dump(ap, w):
                dma("sync", dO, d_dbg.ap()[:, col[0]:col[0] + w], ap)
                col[0] += w
            dump(v_sb.ap().bitcast(F32), 16384)
            dump(ao_sb.ap().bitcast(F32), 1024)
            for G in range(4):
                dump(lq[G].ap().bitcast(F32), 1024)
            W("sync", dO, dO.n)
            dma("sync", dL, jbuf[0][:], d_jlk.ap()[0])
            W("sync", dL, dL.n)
            dump(jbuf[0].ap().bitcast(F32), 2048)
        return _realize(nc, sems, steps, dO)
    ph1b_sched = {0: [2, 3], 1: [4, 5], 2: [6, 7]}
    for si in range(min(NSLOT, slot_limit)):
        emit_ph1_until(NPROF[si])
        mask_load(si)
        if si >= 2:
            mbuf_free[si % 2] = cMK.n
        if si == ONE_TENSOR_SLOTS:
            emit_ph1_until(NT_K)
            W("tensor", cA, ph1V[NT_K - 1])
            W("tensor", cV, cV.n)
        jlk_prefetch(si, 0)
        for G in range(4):
            if G < 3:
                pass
            attn_quad(si, G)
            jbuf_free[G % 2] = cSC.n
            if G < 3:
                jlk_prefetch(si, G + 1)
            if si > 0:
                pieces[G](si - 1)
                if G == 0:
                    guard["ao"] = cT.n
                if DBG and si == 1 and G == 0:
                    dma("sync", dO, d_dbg.ap()[:, 38912:39936],
                        ao_sb.ap().bitcast(F32), waits=[(cV, ev_vals[3])])
                    W("vector", dO, dO.n)
            elif si in ph1b_sched and G in (1, 3):
                ph1b_slot(ph1b_sched[si][0 if G == 1 else 1])
            attn_tail(si, G)
        if si in (1, 2) :
            for tq in ph1b_sched[si]:
                ph1b_slot(tq)
    if slot_limit >= NSLOT:
        for G in range(4):
            pieces[G](NSLOT - 1)
            if G == 0:
                guard["ao"] = cT.n
    if DBG:
        W("sync", cV, cV.n)
        W("sync", cA, cA.n)
        W("sync", cG, cG.n)
        col = [0]

        def dump(ap, w):
            dma("sync", dO, d_dbg.ap()[:, col[0]:col[0] + w], ap)
            col[0] += w
        dump(v_sb.ap().bitcast(F32), 16384)
        dump(ao_sb.ap().bitcast(F32), 1024)
        for G in range(4):
            dump(lq[G].ap().bitcast(F32), 1024)
        dump(p1_sb.ap().bitcast(F32), 256)
        dump(q2_sb.ap().bitcast(F32), 256)
        dump(ln2t.ap().bitcast(F32), 1024)
        dump(x2_sb.ap(), 2048)
        dump(yt_sb.ap(), 2048)
        dump(rb_sb.ap(), 512)
    return _realize(nc, sems, steps, dO)


def _realize(nc, sems, steps, dO):
    semguards = []
    for s in sems:
        g = nc.semaphore(s.name)
        s.h = g.__enter__()
        semguards.append(g)
    final_out = dO.n
    steps["sync"].append(lambda e: e.wait_ge(dO.h, final_out))
    blk = nc.Block()
    block = blk.__enter__()
    for eng_name, meth in (("sync", block.sync), ("scalar", block.scalar),
                           ("vector", block.vector), ("tensor", block.tensor),
                           ("gpsimd", block.gpsimd)):
        def body(e, lst=steps[eng_name]):
            for f in lst:
                f(e)
        meth(body)
    blk.__exit__(None, None, None)
    for g in semguards[::-1]:
        g.__exit__(None, None, None)
    return nc


_NC_CACHE = None


def kernel(L0_attn_x, L0_attn_W1_q, L0_attn_W2_q, L0_attn_W1_k, L0_attn_W2_k,
           L0_attn_W_v, L0_attn_b_v, L0_attn_W_out, L0_attn_b_out,
           L0_ffn_U1, L0_ffn_V1, L0_ffn_b1, L0_ffn_U2, L0_ffn_V2, L0_ffn_b2):
    global _NC_CACHE
    bf = ml_dtypes.bfloat16
    x = np.ascontiguousarray(np.asarray(L0_attn_x, dtype=np.float32))
    B = x.shape[0]

    perm = np.array([(j % 16) * 4 + j // 16 for j in range(64)])
    wq = np.ascontiguousarray(
        np.concatenate([np.asarray(L0_attn_W1_q)[perm],
                        np.asarray(L0_attn_W2_q)[perm]], axis=0).T
        .astype(np.float32))
    wk = np.ascontiguousarray(
        np.concatenate([np.asarray(L0_attn_W1_k)[perm],
                        np.asarray(L0_attn_W2_k)[perm]], axis=0).T
        .astype(np.float32))
    wv = np.ascontiguousarray(np.asarray(L0_attn_W_v, dtype=np.float32).T)
    wout = np.ascontiguousarray(np.asarray(L0_attn_W_out, dtype=np.float32).T)
    bv = np.ascontiguousarray(np.asarray(L0_attn_b_v, dtype=np.float32)[None, :])
    bout = np.ascontiguousarray(
        np.asarray(L0_attn_b_out, dtype=np.float32).reshape(8, 128).T)
    u1 = np.ascontiguousarray(np.asarray(L0_ffn_U1, dtype=np.float32))
    v1 = np.ascontiguousarray(np.asarray(L0_ffn_V1, dtype=np.float32))
    b1 = np.ascontiguousarray(
        np.asarray(L0_ffn_b1, dtype=np.float32).reshape(32, 128).T)
    u2 = np.ascontiguousarray(np.asarray(L0_ffn_U2, dtype=np.float32))
    v2 = np.ascontiguousarray(np.asarray(L0_ffn_V2, dtype=np.float32))
    b2 = np.ascontiguousarray(
        np.asarray(L0_ffn_b2, dtype=np.float32).reshape(8, 128).T)
    ident = np.eye(128, dtype=bf)

    L = np.triu(np.ones((128, 128), dtype=np.float32))
    onesm = np.ones((128, QT), dtype=np.float32)
    zerosm = np.zeros((128, QT), dtype=np.float32)
    tri_a = np.concatenate([L, np.ones((128, 128), dtype=np.float32)], axis=1)
    tri_b = np.concatenate([np.zeros((128, 128), dtype=np.float32), L], axis=1)

    def core_masks(par):
        mk = np.zeros((NSLOT, 4, 128, QT), dtype=np.float32)
        for si in range(NSLOT):
            n = NPROF[si]
            t = TILES[par][si]
            ct = 2 * t + 2
            for j in range(4):
                c = n - 4 + j
                if c < ct - 2:
                    mk[si, j] = onesm
                elif c == ct - 2:
                    mk[si, j] = tri_a
                elif c == ct - 1:
                    mk[si, j] = tri_b
                else:
                    mk[si, j] = zerosm
        return mk.astype(bf)

    masks = {p: core_masks(p) for p in (0, 1)}

    if _NC_CACHE is None:
        _NC_CACHE = build_nc()
    nc = _NC_CACHE

    in_maps = []
    for core in range(8):
        s, par = core // 2, core % 2
        xs = np.ascontiguousarray(x[s])
        xq = np.ascontiguousarray(
            np.concatenate([xs[QT * t:QT * (t + 1)] for t in TILES[par]],
                           axis=0))
        in_maps.append({
            "x": xs, "xq": xq, "masks": masks[par], "ident": ident,
            "wq": wq, "wk": wk, "wv": wv, "bv": bv, "wout": wout,
            "bout": bout, "u1": u1, "v1": v1, "b1": b1, "u2": u2, "v2": v2,
            "b2": b2,
        })

    res = run_bass_kernel_spmd(nc, in_maps, core_ids=list(range(8)))
    y = np.empty((B, T, D), dtype=np.float32)
    for core in range(8):
        s, par = core // 2, core % 2
        yc = res.results[core]["y"]
        for si, t in enumerate(TILES[par]):
            y[s, QT * t:QT * (t + 1)] = yc[QT * si:QT * (si + 1)]
    return y


# version 13
# speedup vs baseline: 1.1854x; 1.0333x over previous
import os as _os
import numpy as np
import ml_dtypes
import concourse.bass as bass
import concourse.mybir as mybir
from concourse.bass_utils import run_bass_kernel_spmd

F32 = mybir.dt.float32
BF16 = mybir.dt.bfloat16
AF = mybir.ActivationFunctionType
OP = mybir.AluOpType
AXX = mybir.AxisListType.X

D = 1024
T = 4096
QT = 256
NSLOT = 8
NPROF = [4, 8, 12, 16, 20, 24, 28, 32]
TILES = {0: [0, 3, 4, 7, 8, 11, 12, 15],
         1: [1, 2, 5, 6, 9, 10, 13, 14]}
SCALE = float(6.0 ** -0.5)
QPAIRS = [(0, 1), (0, 2), (0, 3), (1, 2), (1, 3), (2, 3)]
KPAIRS = [(2, 3), (3, 1), (1, 2), (0, 3), (2, 0), (0, 1)]
NT_K = 32
FF = 4096
RK = 256
ONE_TENSOR_SLOTS = 5


def build_nc(stage_limit=99, slot_limit=8):
    nc = bass.Bass("TRN2", target_bir_lowering=False,
                   detect_race_conditions=False,
                   dynamic_dma_scratch_size=1024)

    d_x = nc.dram_tensor("x", [T, D], F32, kind="ExternalInput")
    d_xq = nc.dram_tensor("xq", [NSLOT * QT, D], F32, kind="ExternalInput")
    d_masks = nc.dram_tensor("masks", [NSLOT, 4, 128, QT], BF16, kind="ExternalInput")
    d_ident = nc.dram_tensor("ident", [128, 128], BF16, kind="ExternalInput")
    d_wq = nc.dram_tensor("wq", [D, 128], F32, kind="ExternalInput")
    d_wk = nc.dram_tensor("wk", [D, 128], F32, kind="ExternalInput")
    d_wv = nc.dram_tensor("wv", [D, D], F32, kind="ExternalInput")
    d_bv = nc.dram_tensor("bv", [1, D], F32, kind="ExternalInput")
    d_wout = nc.dram_tensor("wout", [D, D], F32, kind="ExternalInput")
    d_bout = nc.dram_tensor("bout", [128, 8], F32, kind="ExternalInput")
    d_u1 = nc.dram_tensor("u1", [D, RK], F32, kind="ExternalInput")
    d_v1 = nc.dram_tensor("v1", [RK, FF], F32, kind="ExternalInput")
    d_b1 = nc.dram_tensor("b1", [128, 32], F32, kind="ExternalInput")
    d_u2 = nc.dram_tensor("u2", [FF, RK], F32, kind="ExternalInput")
    d_v2 = nc.dram_tensor("v2", [RK, D], F32, kind="ExternalInput")
    d_b2 = nc.dram_tensor("b2", [128, 8], F32, kind="ExternalInput")
    d_y = nc.dram_tensor("y", [NSLOT * QT, D], F32, kind="ExternalOutput")
    DBG = bool(_os.environ.get("K_DBG"))
    d_jlk = nc.dram_tensor("jlk_s", [4, 128, T], BF16,
                           kind="ExternalOutput" if DBG else "Internal")
    d_dbg = nc.dram_tensor("dbg", [128, 40960], F32, kind="ExternalOutput") if DBG else None
    d_wouts = nc.dram_tensor("wout_s", [8, 128, 1024], BF16, kind="Internal")
    d_u1s = nc.dram_tensor("u1_s", [2, 128, 1024], BF16, kind="Internal")
    d_v1s = nc.dram_tensor("v1_s", [4, 128, 2048], BF16, kind="Internal")
    d_u2s = nc.dram_tensor("u2_s", [2, 128, 4096], BF16, kind="Internal")
    d_v2s = nc.dram_tensor("v2_s", [1, 128, 2048], BF16, kind="Internal")

    steps = {k: [] for k in ("sync", "scalar", "vector", "tensor", "gpsimd")}
    floors = {k: {} for k in steps}

    class Sem:
        __slots__ = ("name", "h", "n")

        def __init__(self, name):
            self.name, self.h, self.n = name, None, 0

    sems = []

    def mksem(name):
        s = Sem(name)
        sems.append(s)
        return s

    dL = mksem("dL")
    dO = mksem("dO")
    dS = mksem("dS")
    cV = mksem("cV")
    cA = mksem("cA")
    cT = mksem("cT")
    cG = mksem("cG")
    cSC = mksem("cSC")
    cEX = mksem("cEX")
    cMK = mksem("cMK")
    cAV = mksem("cAV")

    def em(eng, f):
        steps[eng].append(f)

    def W(eng, sem, v):
        if v <= 0:
            return
        if floors[eng].get(sem.name, 0) >= v:
            return
        floors[eng][sem.name] = v
        em(eng, lambda e, s=sem, vv=v: e.wait_ge(s.h, vv))

    def I(eng, build, incs=(), waits=()):
        for (sem, v) in waits:
            W(eng, sem, v)
        for (sem, k) in incs:
            sem.n += k

        assert len(incs) <= 1, "one inc per instruction"

        def run(e, b=build, incs=tuple(incs)):
            inst = b(e)
            if incs:
                inst.then_inc(incs[0][0].h, incs[0][1])
        em(eng, run)

    def dma(qeng, qsem, out_ap, in_ap, waits=()):
        I(qeng, lambda e, o=out_ap, i=in_ap: e.dma_start(o, i),
          incs=[(qsem, 16)], waits=waits)

    guards = []

    def sb(name, shape, dtype):
        g = nc.sbuf_tensor(name, shape, dtype)
        t = g.__enter__()
        guards.append(g)
        return t

    def psum(name, shape, dtype):
        g = nc.psum_tensor(name, shape, dtype)
        t = g.__enter__()
        guards.append(g)
        return t

    ps_s0 = psum("ps_s0", [128, 1024], F32)
    ps_s1 = psum("ps_s1", [128, 1024], F32)
    ps_nAB = psum("ps_nAB", [128, 512], F32)
    ps_nCD = psum("ps_nCD", [128, 512], F32)
    ps_den = psum("ps_den", [128, 512], F32)
    ps_u = psum("ps_u", [128, 512], F32)

    v_sb = sb("v_sb", [128, NT_K * 1024], BF16)
    lq = [sb(f"lq{G}", [128, NSLOT * QT], BF16) for G in range(4)]
    mbuf = [sb(f"mbuf{i}", [128, 4 * QT], BF16) for i in range(2)]
    p_sb = [sb(f"p{j}", [128, 1024], BF16) for j in range(8)]
    jbuf = [sb(f"jbuf{i}", [128, 4096], BF16) for i in range(2)]
    wbuf = [sb(f"wbuf{i}", [128, 4096], BF16) for i in range(2)]
    wv_sb = sb("wv_sb", [128, 8 * 1024], BF16)
    wq_sb = sb("wq_sb", [128, 8 * 128], BF16)
    wk_sb = sb("wk_sb", [128, 8 * 128], BF16)
    ao_sb = sb("ao_sb", [128, 8 * QT], BF16)
    h_sb = sb("h_sb", [128, 32 * QT], BF16)
    xin = sb("xin", [128, D], F32)
    xnb = sb("xnb", [128, D], BF16)
    lnxt = sb("lnxt", [128, 8 * 128], BF16)
    ln2t = sb("ln2t", [128, 8 * QT], BF16)
    tmbf = [sb(f"tmbf{i}", [128, 512], BF16) for i in range(2)]
    jstage = sb("jstage", [128, 512], BF16)
    e_t1 = sb("e_t1", [128, 128], F32)
    e_t2 = sb("e_t2", [128, 128], F32)
    e_sq = sb("e_sq", [128, 96], F32)
    e_ss = sb("e_ss", [128, 16], F32)
    e_rs = sb("e_rs", [128, 16], F32)
    k12 = sb("k12", [128, 128], F32)
    x2_sb = sb("x2_sb", [128, 2 * D], F32)
    yt_sb = sb("yt_sb", [128, 2 * D], F32)
    ytp = sb("ytp", [128, 256], F32)
    p1_sb = sb("p1_sb", [128, 2 * QT], BF16)
    q2_sb = sb("q2_sb", [128, 2 * QT], BF16)
    stage = [h_sb.ap().bitcast(F32)[:, 0:1024],
             h_sb.ap().bitcast(F32)[:, 1024:2048]]
    bstage = [h_sb.ap()[:, 4096:5120], h_sb.ap()[:, 5120:6144]]
    xqin = sb("xqin", [128, D], F32)
    bn_sb = sb("bn_sb", [128, 12], F32)
    mv_sb = sb("mv_sb", [128, 2], F32)
    rstd_sb = sb("rstd_sb", [128, 1], F32)
    eps_sb = sb("eps_sb", [128, 1], F32)
    eps24_sb = sb("eps24_sb", [128, 1], F32)
    ident_sb = sb("ident_sb", [128, 128], BF16)
    identf_sb = sb("identf_sb", [128, 128], F32)
    ones_bf = sb("ones_bf", [128, 1], BF16)
    onesr = sb("onesr", [128, 128], BF16)
    bv_sb = sb("bv_sb", [1, D], BF16)
    b1_sb = sb("b1_sb", [128, 32], F32)
    bout_sb = sb("bout_sb", [128, 8], F32)
    b2_sb = sb("b2_sb", [128, 8], F32)
    spc_sb = sb("spc", [128, 4], F32)
    rb_sb = sb("rb_sb", [128, 512], F32)
    rd_sb = sb("rd_sb", [128, 1024], BF16)

    wv_view = wv_sb.ap().rearrange("p (c n) -> p c n", c=8)
    s1bf = ps_s1.ap().bitcast(BF16)
    ubf = ps_u.ap().bitcast(BF16)

    def mm(out, lhsT, rhs, start=True, stop=True, tp=None, incs=(), waits=()):
        I("tensor",
          lambda e, o=out, l=lhsT, r=rhs, st=start, sp=stop, tpp=tp:
          e.matmul(o, l, r, start=st, stop=sp, tile_position=tpp,
                   skip_group_check=True),
          incs=incs, waits=waits)

    def tr(out, in_, incs=(), waits=()):
        I("tensor",
          lambda e, o=out, i=in_, idn=(identf_sb if in_.dtype == F32 else ident_sb):
          e.transpose(o, i, idn[:]),
          incs=incs, waits=waits)

    def act(out, in_, func, bias=0.0, scale=1.0, incs=(), waits=()):
        I("scalar",
          lambda e, o=out, i=in_, f=func, b=bias, s=scale:
          e.activation(o, i, f, bias=b, scale=s),
          incs=incs, waits=waits)

    guard = {"xin": 0, "xnb": 0, "tmbf0": 0, "tmbf1": 0, "ao": 0}

    def vfence():
        W("vector", cV, cV.n)

    def ln_tile(src_ap, dst_ap, dst_guard_T=0):
        I("vector", lambda e, s=src_ap: e.bn_stats(bn_sb[:, 0:6], s[:, 0:512]))
        I("vector", lambda e, s=src_ap: e.bn_stats(bn_sb[:, 6:12], s[:, 512:1024]),
          incs=[(cV, 1)])
        vfence()
        I("vector", lambda e: e.bn_aggr(mv_sb[:], bn_sb[:]), incs=[(cV, 1)])
        act(rstd_sb[:], mv_sb[:, 1:2], AF.Ln, bias=eps_sb[:], waits=[(cV, cV.n)])
        act(rstd_sb[:], rstd_sb[:], AF.Exp, scale=-0.5, incs=[(cA, 1)])
        I("vector",
          lambda e, s=src_ap, d=dst_ap: e.tensor_scalar(
              d, s, mv_sb[:, 0:1], rstd_sb[:], op0=OP.subtract, op1=OP.mult),
          incs=[(cV, 1)], waits=[(cA, cA.n), (cT, dst_guard_T)])

    def exterior(src_ap, pairs, dst_ap, dst_guard_T=0):
        def v1c(c):
            return src_ap[:, c * 16:c * 16 + 16].rearrange(
                "p (g a) -> p g a", g=4).unsqueeze(3)

        def v2c(c):
            return src_ap[:, 64 + c * 16:64 + c * 16 + 16].rearrange(
                "p (g a) -> p g a", g=4).unsqueeze(3)

        t1v = e_t1.ap().rearrange("p (g a x) -> p g a x", g=4, a=4, x=8)
        t2v = e_t2.ap().rearrange("p (g a x) -> p g a x", g=4, a=4, x=8)
        vfence()
        for p, (a, b) in enumerate(pairs):
            I("vector", lambda e, o=t1v[:, :, :, p:p + 1], x=v1c(a), y=v2c(b):
              e.tensor_tensor(o, x, y, op=OP.mult))
            I("vector", lambda e, o=t2v[:, :, :, p:p + 1], x=v1c(b), y=v2c(a):
              e.tensor_tensor(o, x, y, op=OP.mult))
        Lsp = t1v[:, :, :, 0:6]
        I("vector", lambda e, L=Lsp: e.tensor_tensor(L, L, t2v[:, :, :, 0:6],
                                                     op=OP.subtract))
        sqv = e_sq.ap().rearrange("p (g a x) -> p g a x", g=4, a=4, x=6)
        I("vector", lambda e, L=Lsp, s=sqv: e.tensor_tensor(s, L, L, op=OP.mult))
        I("vector", lambda e, s=sqv: e.reduce_sum(
            e_ss.ap().rearrange("p (g a) -> p g a", g=4).unsqueeze(3),
            s, axis=AXX), incs=[(cV, 1)])
        act(e_rs[:], e_ss[:], AF.Ln, bias=eps24_sb[:], waits=[(cV, cV.n)])
        act(e_rs[:], e_rs[:], AF.Exp, scale=-0.5, incs=[(cA, 1)])
        rsb = e_rs.ap().rearrange("p (g a) -> p g a", g=4).unsqueeze(3) \
            .broadcast_to([128, 4, 4, 6])
        I("vector", lambda e, d=dst_ap, L=Lsp, r=rsb: e.tensor_tensor(
            d.rearrange("p (g a x) -> p g a x", g=4, a=4)[:, :, :, 0:6],
            L, r, op=OP.mult),
          incs=[(cV, 1)], waits=[(cA, cA.n), (cT, dst_guard_T)])

    castv = [0, 0]
    bst_free = [0, 0]

    def spill(piece_idx, in_ap, out_ap, to_sbuf=None):
        b = piece_idx % 2
        dma("scalar", dS, stage[b][:, 0:in_ap.size() // 128], in_ap,
            waits=[(cV, castv[b])])
        dv = dS.n
        w = out_ap.size() // 128 if out_ap is not None else to_sbuf.size() // 128
        if to_sbuf is not None:
            I("vector", lambda e, o=to_sbuf, w=w: e.tensor_copy(
                o, stage[b][:, 0:w]), incs=[(cV, 1)], waits=[(dS, dv)])
            castv[b] = cV.n
        else:
            I("vector", lambda e, w=w: e.tensor_copy(
                bstage[b][:, 0:w], stage[b][:, 0:w]),
              incs=[(cV, 1)], waits=[(dS, dv)])
            castv[b] = cV.n
            dma("scalar", dS, out_ap, bstage[b].rearrange(
                "p (a n) -> p a n", a=out_ap.shape[-2] if out_ap.ndim == 3 else 1)
                if False else bstage[b][:, 0:w],
                waits=[(cV, cV.n)])

    def startup():
        I("vector", lambda e: e.memset(eps_sb[:], 1e-5))
        I("vector", lambda e: e.memset(eps24_sb[:], 1e-24))
        I("vector", lambda e: e.memset(ones_bf[:], 1.0))
        I("vector", lambda e: e.memset(onesr[:], 1.0))
        I("vector", lambda e: e.memset(e_t1[:], 0.0))
        I("vector", lambda e: e.memset(e_t2[:], 0.0))
        I("vector", lambda e: e.memset(tmbf[0][:], 0.0))
        I("vector", lambda e: e.memset(tmbf[1][:], 0.0), incs=[(cV, 1)])
        dma("sync", dL, ident_sb[:], d_ident.ap())
        I("vector", lambda e: e.tensor_copy(identf_sb[:], ident_sb[:]),
          incs=[(cV, 1)], waits=[(dL, dL.n)])

        dma("sync", dL, b1_sb[:], d_b1.ap())
        dma("sync", dL, bout_sb[:], d_bout.ap())
        dma("sync", dL, b2_sb[:], d_b2.ap())
        pi = 0
        spill(pi, d_wq.ap().rearrange("(r p) n -> p r n", p=128), None,
              to_sbuf=wq_sb[:]); pi += 1
        spill(pi, d_wk.ap().rearrange("(r p) n -> p r n", p=128), None,
              to_sbuf=wk_sb[:]); pi += 1
        dma("scalar", dS, stage[0][0:1, :], d_bv.ap(), waits=[(cV, castv[0])])
        I("vector", lambda e: e.tensor_copy(bv_sb[:], stage[0][0:1, :]),
          incs=[(cV, 1)], waits=[(dS, dS.n)])
        castv[0] = cV.n
        for r in range(8):
            spill(pi, d_wv.ap()[r * 128:(r + 1) * 128, :], None,
                  to_sbuf=wv_sb[:, r * 1024:(r + 1) * 1024]); pi += 1
        for j in range(8):
            spill(pi, d_wout.ap().rearrange("(r p) n -> p r n", p=128)
                  [:, :, j * 128:(j + 1) * 128], d_wouts.ap()[j]); pi += 1
        for m in range(2):
            spill(pi, d_u1.ap().rearrange("(r p) n -> p r n", p=128)
                  [:, :, m * 128:(m + 1) * 128], d_u1s.ap()[m]); pi += 1
        for q in range(4):
            for rh in range(2):
                spill(pi, d_v1.ap()[rh * 128:(rh + 1) * 128,
                                    q * 1024:(q + 1) * 1024],
                      d_v1s.ap()[q].rearrange("p (c x n) -> p c x n", c=8, x=2)
                      [:, :, rh, :]); pi += 1
        for m in range(2):
            for cg in range(4):
                spill(pi, d_u2.ap().rearrange("(c p) n -> p c n", p=128)
                      [:, cg * 8:(cg + 1) * 8, m * 128:(m + 1) * 128],
                      d_u2s.ap()[m][:, cg * 1024:(cg + 1) * 1024]); pi += 1
        for m in range(2):
            spill(pi, d_v2.ap()[m * 128:(m + 1) * 128, :],
                  d_v2s.ap()[0].rearrange("p (j x n) -> p j x n", j=8, x=2)
                  [:, :, m, :]); pi += 1

    ph1V = []
    ph1J = []

    def ph1_tile(t):
        dma("sync", dL, xin[:], d_x.ap()[t * 128:(t + 1) * 128, :],
            waits=[(cV, guard["xin"])])
        W("vector", dL, dL.n)
        ln_tile(xin.ap(), xnb.ap(), dst_guard_T=guard["xnb"])
        guard["xin"] = cV.n
        lnv = cV.n
        for r in range(8):
            tr(s1bf[:, r * 128:(r + 1) * 128], xnb[:, r * 128:(r + 1) * 128],
               waits=([(cV, lnv), (cA, cA.n)] if r == 0 else ()),
               incs=([(cT, 1)] if r == 7 else ()))
        guard["xnb"] = cT.n
        I("vector", lambda e: e.tensor_copy(lnxt[:], s1bf[:, 0:1024]),
          incs=[(cV, 1)], waits=[(cT, cT.n)])
        lnxtv = cV.n
        for hh in range(2):
            dst = ps_s1[:, hh * 512:(hh + 1) * 512]
            for r in range(8):
                mm(dst, lnxt[:, r * 128:(r + 1) * 128],
                   wv_view[:, r, hh * 512:(hh + 1) * 512],
                   start=(r == 0), stop=False,
                   waits=([(cV, lnxtv), (cA, cA.n)] if r == 0 else ()))
            mm(dst, onesr[0:1, :], bv_sb[0:1, hh * 512:(hh + 1) * 512],
               start=False, stop=True, incs=([(cT, 1)] if hh == 1 else ()))
        act(v_sb[:, t * 1024:(t + 1) * 1024], ps_s1[:], AF.Copy,
            incs=[(cA, 1)], waits=[(cT, cT.n)])
        ph1V.append(cA.n)
        for r in range(8):
            mm(ps_s1[:, 0:128], lnxt[:, r * 128:(r + 1) * 128],
               wk_sb[:, r * 128:(r + 1) * 128],
               start=(r == 0), stop=(r == 7),
               waits=([(cA, cA.n)] if r == 0 else ()),
               incs=([(cT, 1)] if r == 7 else ()))
        I("vector", lambda e: e.tensor_copy(k12[:], ps_s1[:, 0:128]),
          incs=[(cV, 1)], waits=[(cT, cT.n)])
        buf = t % 2
        exterior(k12.ap(), KPAIRS, tmbf[buf].ap(),
                 dst_guard_T=guard[f"tmbf{buf}"])
        extv = cV.n
        for G in range(4):
            tr(s1bf[:, 1024 + G * 128:1024 + (G + 1) * 128],
               tmbf[buf][:, G * 128:(G + 1) * 128],
               waits=([(cV, extv), (cA, cA.n)] if G == 0 else ()),
               incs=([(cT, 1)] if G == 3 else ()))
        guard[f"tmbf{buf}"] = cT.n
        I("vector", lambda e: e.tensor_copy(jstage[:], s1bf[:, 1024:1536]),
          incs=[(cV, 1)], waits=[(cT, cT.n), (dO, (ph1J[-1] if ph1J else 0))])
        dma("sync", dO,
            d_jlk.ap()[:, :, t * 128:(t + 1) * 128].transpose([1, 0, 2]),
            jstage.ap().rearrange("p (g n) -> p g n", g=4),
            waits=[(cV, cV.n)])
        ph1J.append(dO.n)

    ph1Q = [0] * NSLOT

    def ph1b_slot(si):
        for tt in range(2):
            tq = si * 2 + tt
            dma("sync", dL, xin[:], d_xq.ap()[tq * 128:(tq + 1) * 128, :],
                waits=[(cV, guard["xin"])])
            W("vector", dL, dL.n)
            ln_tile(xin.ap(), xnb.ap(), dst_guard_T=guard["xnb"])
            guard["xin"] = cV.n
            lnv = cV.n
            for r in range(8):
                tr(s1bf[:, r * 128:(r + 1) * 128], xnb[:, r * 128:(r + 1) * 128],
                   waits=([(cV, lnv), (cA, cA.n)] if r == 0 else ()),
                   incs=([(cT, 1)] if r == 7 else ()))
            guard["xnb"] = cT.n
            I("vector", lambda e: e.tensor_copy(lnxt[:], s1bf[:, 0:1024]),
              incs=[(cV, 1)], waits=[(cT, cT.n)])
            lnxv = cV.n
            for r in range(8):
                mm(ps_s1[:, 0:128], lnxt[:, r * 128:(r + 1) * 128],
                   wq_sb[:, r * 128:(r + 1) * 128],
                   start=(r == 0), stop=(r == 7),
                   waits=([(cV, lnxv), (cA, cA.n)] if r == 0 else ()),
                   incs=([(cT, 1)] if r == 7 else ()))
            I("vector", lambda e: e.tensor_copy(k12[:], ps_s1[:, 0:128]),
              incs=[(cV, 1)], waits=[(cT, cT.n)])
            buf = tt
            exterior(k12.ap(), QPAIRS, tmbf[buf].ap(),
                     dst_guard_T=guard[f"tmbf{buf}"])
            extv = cV.n
            for G in range(4):
                tr(s1bf[:, 1024 + G * 128:1024 + (G + 1) * 128],
                   tmbf[buf][:, G * 128:(G + 1) * 128],
                   waits=([(cV, extv), (cA, cA.n)] if G == 0 else ()),
                   incs=([(cT, 1)] if G == 3 else ()))
            guard[f"tmbf{buf}"] = cT.n
            for G in range(4):
                I("vector", lambda e, G=G, si=si, tt=tt: e.tensor_copy(
                    lq[G][:, si * QT + tt * 128: si * QT + (tt + 1) * 128],
                    s1bf[:, 1024 + G * 128:1024 + (G + 1) * 128]),
                  incs=([(cV, 1)] if G == 3 else ()), waits=[(cT, cT.n)])
        ph1Q[si] = cV.n

    ggi = [0]
    quad_no = [0]
    s_free = {0: (0, 0), 1: (0, 0)}
    jbuf_free = [0, 0]
    jd = {}
    exp_after = {}
    mask_after = {}

    mload = [0] * NSLOT

    def mask_load(si):
        dma("sync", dL, mbuf[si % 2].ap().rearrange("p (j q) -> p j q", j=4),
            d_masks.ap()[si].transpose([1, 0, 2]),
            waits=[(cMK, mbuf_free[si % 2])])
        mload[si] = dL.n

    mbuf_free = [0, 0]

    def jlk_prefetch(si, G):
        n = NPROF[si]
        b = G % 2
        dma("sync", dL, jbuf[b][:, 0:n * 128], d_jlk.ap()[G][:, 0:n * 128],
            waits=[(cSC, jbuf_free[b]), (dO, ph1J[n - 1])])
        jd[(si, G)] = dL.n

    def emit_av(si, G, my_quad, gj, h_ggi, chunks, masked, ngrp):
        slabs = [p_sb[(h_ggi % 2) * 4 + a] for a in range(4)]
        first = gj == 0
        last = gj == ngrp - 1
        for ci, c in enumerate(chunks):
            w = []
            if ci == 0:
                for a in range(4):
                    if masked:
                        w.append((cMK, mask_after[(h_ggi, a)]))
                    else:
                        w.append((cEX, exp_after[(h_ggi, a)]))
                w.append((cA, ph1V[max(chunks)]))
                if first and my_quad >= 1:
                    w.append((cV, ev_vals[my_quad - 1]))
            st = first and ci == 0
            sp = last and ci == 3
            for a in range(4):
                nb = ps_nAB if a < 2 else ps_nCD
                vcol = c * 1024 + (4 * G + a) * 64
                base = 64 * (a % 2)
                mm(nb[base:base + 64, 0:256],
                   v_sb[:, vcol:vcol + 64],
                   slabs[a][:, ci * 256:(ci + 1) * 256],
                   start=st, stop=sp, tp=(0, base),
                   waits=(w if a == 0 else ()))
            dendst = [ps_nAB[0:1, 256:512], ps_den[0:1, 0:256],
                      ps_nCD[0:1, 256:512], ps_den[0:1, 256:512]]
            for a in range(4):
                mm(dendst[a], ones_bf[:],
                   slabs[a][:, ci * 256:(ci + 1) * 256],
                   start=(st if a == 1 else False), stop=sp, tp=(0, 0),
                   incs=([(cAV, 1)] if (a == 3 and ci == 3) else ()))

    def attn_quad(si, G):
        n = NPROF[si]
        ngrp = n // 4
        two_t = si >= ONE_TENSOR_SLOTS
        grp = [list(range(n - 4, n))] + \
              [list(range(4 * g, 4 * g + 4)) for g in range(ngrp - 1)]
        qv = ph1Q[si]
        my_quad = quad_no[0]
        quad_no[0] += 1
        jdv = jd[(si, G)]
        pend = []
        for gi, chunks in enumerate(grp):
            pull_fill()
            masked = gi == 0
            g_ggi = ggi[0]
            ggi[0] += 1
            slabs = [p_sb[(g_ggi % 2) * 4 + a] for a in range(4)]
            ring_wait = max(0, g_ggi - 1)
            sc_done = {}

            def sc_head(a, st):
                stv = ps_s0 if st == 0 else ps_s1
                for ci, c in enumerate(chunks):
                    w = ()
                    if ci == 0:
                        w = [(dL, jdv), (cV, qv), (cEX, s_free[st][0])]
                    elif ci == 2:
                        w = [(cEX, s_free[st][1])]
                    mm(stv[:, ci * 256:(ci + 1) * 256],
                       jbuf[G % 2][32 * a:32 * a + 6, c * 128:(c + 1) * 128],
                       lq[G][32 * a:32 * a + 6, si * QT:(si + 1) * QT],
                       tp=(32 * a, 0),
                       waits=w,
                       incs=([(cSC, 1)] if ci in (1, 3) else ()))
                sc_done[a] = cSC.n

            def exp_head(a, st):
                stv = ps_s0 if st == 0 else ps_s1
                act(slabs[a][:, 0:512], stv[:, 0:512], AF.Exp, scale=SCALE,
                    waits=[(cSC, sc_done[a] - 1), (cAV, ring_wait)],
                    incs=[(cEX, 1)])
                h0 = cEX.n
                act(slabs[a][:, 512:1024], stv[:, 512:1024], AF.Exp,
                    scale=SCALE, waits=[(cSC, sc_done[a])],
                    incs=[(cEX, 1)])
                exp_after[(g_ggi, a)] = cEX.n
                s_free[st] = (h0, cEX.n)

            if two_t:
                sc_head(0, 0); sc_head(2, 1)
                exp_head(0, 0); exp_head(2, 1)
                sc_head(1, 0); sc_head(3, 1)
                exp_head(1, 0); exp_head(3, 1)
            else:
                for a in range(4):
                    sc_head(a, 0)
                    exp_head(a, 0)
            if masked:
                for a in range(4):
                    I("vector", lambda e, sl=slabs[a], si=si: e.tensor_tensor(
                        sl[:], sl[:], mbuf[si % 2][:], op=OP.mult),
                      waits=[(cEX, exp_after[(g_ggi, a)]), (dL, mload[si])],
                      incs=[(cMK, 1)])
                    mask_after[(g_ggi, a)] = cMK.n
            pend.append((gi, g_ggi, chunks, masked))
            if gi >= 1:
                gj, h, chs, msk = pend.pop(0)
                emit_av(si, G, my_quad, gj, h, chs, msk, ngrp)
        for (gj, h, chs, msk) in pend:
            emit_av(si, G, my_quad, gj, h, chs, msk, ngrp)

    def attn_tail(si, G):
        avv = ggi[0]
        densrc = [ps_nAB[0:1, 256:512], ps_den[0:1, 0:256],
                  ps_nCD[0:1, 256:512], ps_den[0:1, 256:512]]
        def _recip(e, a, s):
            with nc.allow_low_precision(reason="recip cast to bf16 for matmul"
                                        " rhs; same precision as prior"
                                        " fp32-recip + bf16-copy path"):
                return e.reciprocal(rd_sb[0:1, 256 * a:256 * (a + 1)], s)
        for a in range(4):
            I("vector", lambda e, a=a, s=densrc[a]: _recip(e, a, s),
              waits=([(cAV, avv)] if a == 0 else ()),
              incs=([(cV, 1)] if a == 3 else ()))
        rdv = cV.n
        bc = [(0, ps_u[0:32, 0:256], (0, 0)), (0, ps_u[32:64, 0:256], (0, 32)),
              (1, ps_u[64:96, 0:256], (0, 64)), (1, ps_u[96:128, 0:256], (0, 96)),
              (2, ps_u[0:32, 256:512], (0, 0)), (2, ps_u[32:64, 256:512], (0, 32)),
              (3, ps_u[64:96, 256:512], (0, 64)), (3, ps_u[96:128, 256:512], (0, 96))]
        for i, (a, dst, tpp) in enumerate(bc):
            mm(dst, onesr[0:1, 0:32], rd_sb[0:1, 256 * a:256 * (a + 1)],
               tp=tpp,
               waits=([(cV, max(rdv, b7V[0])), (cA, b7A[0])] if i == 0 else ()),
               incs=([(cT, 1)] if i == 7 else ()))
        bcv = cT.n
        I("vector", lambda e: e.tensor_copy(rb_sb[:], ps_u[:]),
          waits=[(cT, bcv)], incs=[(cV, 1)])
        b7V[0] = cV.n
        for a in range(4):
            h = 4 * G + a
            nsrc = ps_nAB if a < 2 else ps_nCD
            base = 64 * (a % 2)
            ucol = 0 if a < 2 else 256
            I("vector", lambda e, h=h, nsrc=nsrc, base=base, ucol=ucol:
              e.tensor_tensor(
                  ao_sb[64 * (h % 2):64 * (h % 2) + 64,
                        (h // 2) * QT:(h // 2 + 1) * QT],
                  nsrc[base:base + 64, 0:256],
                  rb_sb[base:base + 64, ucol:ucol + 256], op=OP.mult),
              waits=[(cT, guard["ao"])],
              incs=([(cV, 1)] if a == 3 else ()))
        ev_vals.append(cV.n)

    wbuf_free = [0, 0]
    ev_vals = []
    b7A = [0]
    b7V = [0]
    wpc = [0]
    h_guard = [0]
    h_acts = {}
    out_dma = [0]
    spill_done = [0]

    def wpiece(dram_ap, ncols):
        b = wpc[0] % 2
        wpc[0] += 1
        dma("scalar", dS, wbuf[b][:, 0:ncols], dram_ap,
            waits=[(cT, wbuf_free[b]), (dS, spill_done[0])])
        return b, dS.n

    def piece_P0(si):
        for j in range(8):
            yield
            b, dv = wpiece(d_wouts.ap()[j], 1024)
            for rr in range(8):
                mm(ps_u[:, 0:256], wbuf[b][:, rr * 128:(rr + 1) * 128],
                   ao_sb[:, rr * QT:(rr + 1) * QT],
                   start=(rr == 0), stop=(rr == 7),
                   waits=([(dS, dv), (cV, ev_vals[4 * (si + 1) - 1]),
                           (cA, b7A[0]), (cV, b7V[0])]
                          if rr == 0 else ()),
                   incs=([(cT, 1)] if rr == 7 else ()))
            wbuf_free[b] = cT.n
            act(ytp[:], ps_u[:, 0:256], AF.Identity, bias=bout_sb[:, j:j + 1],
                incs=[(cA, 1)], waits=[(cT, cT.n)])
            b7A[0] = cA.n
            for tt in range(2):
                tr(ps_u[:, tt * 128:(tt + 1) * 128], ytp[:, tt * 128:(tt + 1) * 128],
                   waits=([(cA, cA.n)] if tt == 0 else ()),
                   incs=([(cT, 1)] if tt == 1 else ()))
            I("vector", lambda e, j=j: e.tensor_copy(
                yt_sb.ap().rearrange("p (t n) -> p t n", t=2)
                [:, :, j * 128:(j + 1) * 128],
                ps_u.ap()[:, 0:256].rearrange("p (t n) -> p t n", t=2)),
              incs=[(cV, 1)], waits=[(cT, cT.n)])
            b7V[0] = cV.n

    def piece_P1(si):
        for tt in range(2):
            yield
            dma("sync", dL, xqin[:], d_xq.ap()[(si * 2 + tt) * 128:
                                               (si * 2 + tt + 1) * 128, :],
                waits=[(cG, cG.n)])
            ldv = dL.n
            yv = yt_sb.ap()[:, tt * 1024:(tt + 1) * 1024]
            ln_tile(yv, yv)
            lnv = cV.n
            I("gpsimd", lambda e, tt=tt, yv=yv: e.tensor_add(
                x2_sb[:, tt * 1024:(tt + 1) * 1024], xqin[:], yv),
              waits=[(cV, lnv), (dL, ldv), (dO, out_dma[0])], incs=[(cG, 1)])
            W("vector", cG, cG.n)
            yield
            ln_tile(x2_sb.ap()[:, tt * 1024:(tt + 1) * 1024], xnb.ap(),
                    dst_guard_T=guard["xnb"])
            lnv2 = cV.n
            for half in range(2):
                for r in range(4):
                    tr(ubf[:, half * 512 + r * 128: half * 512 + (r + 1) * 128],
                       xnb[:, half * 512 + r * 128: half * 512 + (r + 1) * 128],
                       waits=([(cV, max(lnv2, b7V[0])), (cA, b7A[0])]
                              if r == 0 else ()),
                       incs=([(cT, 1)] if r == 3 else ()))
                if half == 1:
                    guard["xnb"] = cT.n
                I("vector", lambda e, tt=tt, half=half: e.tensor_copy(
                    ln2t.ap().rearrange("p (r q) -> p r q", r=8)
                    [:, half * 4:(half + 1) * 4, tt * 128:(tt + 1) * 128],
                    ubf.rearrange("p (r q) -> p r q", r=8)
                    [:, half * 4:(half + 1) * 4, :]),
                  incs=[(cV, 1)], waits=[(cT, cT.n)])
                b7V[0] = cV.n

    def piece_P2(si):
        l2v = cV.n
        for m in range(2):
            yield
            b, dv = wpiece(d_u1s.ap()[m], 1024)
            for rr in range(8):
                mm(ps_u[:, 0:256], wbuf[b][:, rr * 128:(rr + 1) * 128],
                   ln2t[:, rr * QT:(rr + 1) * QT],
                   start=(rr == 0), stop=(rr == 7),
                   waits=([(dS, dv), (cV, max(l2v, b7V[0])), (cA, b7A[0])]
                          if rr == 0 else ()),
                   incs=([(cT, 1)] if rr == 7 else ()))
            wbuf_free[b] = cT.n
            act(p1_sb[:, m * QT:(m + 1) * QT], ps_u[:, 0:256], AF.Copy,
                incs=[(cA, 1)], waits=[(cT, cT.n)])
            b7A[0] = cA.n
        p1v = cA.n
        for q in range(4):
            yield
            b, dv = wpiece(d_v1s.ap()[q], 2048)
            for cc8 in range(8):
                cc = q * 8 + cc8
                for rh in range(2):
                    mm(ps_u[:, 0:256],
                       wbuf[b][:, cc8 * 256 + rh * 128: cc8 * 256 + (rh + 1) * 128],
                       p1_sb[:, rh * QT:(rh + 1) * QT],
                       start=(rh == 0), stop=(rh == 1),
                       waits=([(dS, dv), (cA, max(p1v, b7A[0])), (cV, b7V[0])]
                              if rh == 0 else ()),
                       incs=([(cT, 1)] if rh == 1 else ()))
                act(h_sb[:, cc * QT:(cc + 1) * QT], ps_u[:, 0:256],
                    AF.Gelu_apprx_tanh, bias=b1_sb[:, cc:cc + 1],
                    incs=[(cA, 1)],
                    waits=[(cT, cT.n), (cT, h_guard[0])])
                b7A[0] = cA.n
                h_acts[cc] = cA.n
            wbuf_free[b] = cT.n

    def piece_P3(si):
        for m in range(2):
            yield
            b, dv = wpiece(d_u2s.ap()[m], 4096)
            for cc in range(32):
                if cc == 16:
                    yield
                w = [(cA, h_acts[cc])] if m == 0 else []
                if cc == 0:
                    w += [(dS, dv), (cA, b7A[0]), (cV, b7V[0])]
                mm(ps_u[:, 0:256], wbuf[b][:, cc * 128:(cc + 1) * 128],
                   h_sb[:, cc * QT:(cc + 1) * QT],
                   start=(cc == 0), stop=(cc == 31),
                   waits=w, incs=([(cT, 1)] if cc == 31 else ()))
            wbuf_free[b] = cT.n
            act(q2_sb[:, m * QT:(m + 1) * QT], ps_u[:, 0:256], AF.Copy,
                incs=[(cA, 1)], waits=[(cT, cT.n)])
            b7A[0] = cA.n
        h_guard[0] = cT.n
        q2v = cA.n
        b, dv = wpiece(d_v2s.ap()[0], 2048)
        for j in range(8):
            if j % 2 == 0:
                yield
            for m in range(2):
                mm(ps_u[:, 0:256],
                   wbuf[b][:, j * 256 + m * 128: j * 256 + (m + 1) * 128],
                   q2_sb[:, m * QT:(m + 1) * QT],
                   start=(m == 0), stop=(m == 1),
                   waits=([(dS, dv), (cA, max(q2v, b7A[0])), (cV, b7V[0])]
                          if m == 0 else ()),
                   incs=([(cT, 1)] if m == 1 else ()))
            act(ytp[:], ps_u[:, 0:256], AF.Identity, bias=b2_sb[:, j:j + 1],
                incs=[(cA, 1)], waits=[(cT, cT.n)])
            b7A[0] = cA.n
            for tt in range(2):
                tr(ps_u[:, tt * 128:(tt + 1) * 128],
                   ytp[:, tt * 128:(tt + 1) * 128],
                   waits=([(cA, cA.n)] if tt == 0 else ()),
                   incs=([(cT, 1)] if tt == 1 else ()))
            I("vector", lambda e, j=j: e.tensor_copy(
                yt_sb.ap().rearrange("p (t n) -> p t n", t=2)
                [:, :, j * 128:(j + 1) * 128],
                ps_u.ap()[:, 0:256].rearrange("p (t n) -> p t n", t=2)),
              incs=[(cV, 1)], waits=[(cT, cT.n)])
            b7V[0] = cV.n
        wbuf_free[b] = cT.n
        for tt in range(2):
            yield
            yv = yt_sb.ap()[:, tt * 1024:(tt + 1) * 1024]
            ln_tile(yv, yv)
            lnv = cV.n
            I("gpsimd", lambda e, tt=tt, yv=yv: e.tensor_add(
                x2_sb[:, tt * 1024:(tt + 1) * 1024],
                x2_sb[:, tt * 1024:(tt + 1) * 1024], yv),
              waits=[(cV, lnv)], incs=[(cG, 1)])
            dma("sync", dO,
                d_y.ap()[(si * 2 + tt) * 128:(si * 2 + tt + 1) * 128, :],
                x2_sb[:, tt * 1024:(tt + 1) * 1024], waits=[(cG, cG.n)])
        out_dma[0] = dO.n

    def piece_rest(si):
        yield from piece_P1(si)
        yield from piece_P2(si)
        yield from piece_P3(si)

    fill_gen = [None, None]

    def pull_fill():
        for k in (0, 1):
            g = fill_gen[k]
            if g is None:
                continue
            try:
                next(g)
            except StopIteration:
                fill_gen[k] = None
                continue
            return

    def drain_fill(k):
        g = fill_gen[k]
        if g is None:
            return
        for _ in g:
            pass
        fill_gen[k] = None

    startup()
    spill_done[0] = dS.n
    W("tensor", dL, dL.n)
    if stage_limit < 1:
        return _realize(nc, sems, steps, dO)
    W("tensor", cV, cV.n)
    W("scalar", dL, dL.n)
    W("vector", dL, dL.n)
    W("gpsimd", dL, dL.n)
    ph1b_slot(0)
    ph1b_slot(1)
    kt = [0]
    if stage_limit < 2:
        return _realize(nc, sems, steps, dO)

    def emit_ph1_until(ntiles):
        while kt[0] < ntiles:
            ph1_tile(kt[0])
            kt[0] += 1

    if stage_limit < 3:
        for tq in range(2, 8):
            ph1b_slot(tq)
        emit_ph1_until(NT_K)
        if DBG:
            W("sync", cV, cV.n)
            W("sync", cA, cA.n)
            col = [0]

            def dump(ap, w):
                dma("sync", dO, d_dbg.ap()[:, col[0]:col[0] + w], ap)
                col[0] += w
            dump(v_sb.ap().bitcast(F32), 16384)
            dump(ao_sb.ap().bitcast(F32), 1024)
            for G in range(4):
                dump(lq[G].ap().bitcast(F32), 1024)
            W("sync", dO, dO.n)
            dma("sync", dL, jbuf[0][:], d_jlk.ap()[0])
            W("sync", dL, dL.n)
            dump(jbuf[0].ap().bitcast(F32), 2048)
        return _realize(nc, sems, steps, dO)
    ph1b_sched = {0: [2, 3], 1: [4, 5], 2: [6, 7]}
    for si in range(min(NSLOT, slot_limit)):
        emit_ph1_until(NPROF[si])
        mask_load(si)
        if si >= 2:
            mbuf_free[si % 2] = cMK.n
        if si == ONE_TENSOR_SLOTS:
            emit_ph1_until(NT_K)
            W("tensor", cA, ph1V[NT_K - 1])
            W("tensor", cV, cV.n)
        jlk_prefetch(si, 0)
        for G in range(4):
            if si > 0 and G == 0:
                fill_gen[0] = piece_P0(si - 1)
            if si > 0 and G == 1:
                fill_gen[1] = piece_P1(si - 1)
            if si > 0 and G == 2:
                fill_gen[1] = piece_P2(si - 1)
            if si > 0 and G == 3:
                fill_gen[1] = piece_P3(si - 1)
            attn_quad(si, G)
            jbuf_free[G % 2] = cSC.n
            if G < 3:
                jlk_prefetch(si, G + 1)
            if si > 0:
                if G == 0:
                    drain_fill(0)
                    guard["ao"] = cT.n
                    if DBG and si == 1:
                        dma("sync", dO, d_dbg.ap()[:, 38912:39936],
                            ao_sb.ap().bitcast(F32), waits=[(cV, ev_vals[3])])
                        W("vector", dO, dO.n)
                if G >= 1:
                    drain_fill(1)
            elif si in ph1b_sched and G in (1, 3):
                ph1b_slot(ph1b_sched[si][0 if G == 1 else 1])
            attn_tail(si, G)
        if si in (1, 2) :
            for tq in ph1b_sched[si]:
                ph1b_slot(tq)
    if slot_limit >= NSLOT:
        fill_gen[0] = piece_P0(NSLOT - 1)
        drain_fill(0)
        guard["ao"] = cT.n
        fill_gen[1] = piece_rest(NSLOT - 1)
        drain_fill(1)
    if DBG:
        W("sync", cV, cV.n)
        W("sync", cA, cA.n)
        W("sync", cG, cG.n)
        col = [0]

        def dump(ap, w):
            dma("sync", dO, d_dbg.ap()[:, col[0]:col[0] + w], ap)
            col[0] += w
        dump(v_sb.ap().bitcast(F32), 16384)
        dump(ao_sb.ap().bitcast(F32), 1024)
        for G in range(4):
            dump(lq[G].ap().bitcast(F32), 1024)
        dump(p1_sb.ap().bitcast(F32), 256)
        dump(q2_sb.ap().bitcast(F32), 256)
        dump(ln2t.ap().bitcast(F32), 1024)
        dump(x2_sb.ap(), 2048)
        dump(yt_sb.ap(), 2048)
        dump(rb_sb.ap(), 512)
    return _realize(nc, sems, steps, dO)


def _realize(nc, sems, steps, dO):
    semguards = []
    for s in sems:
        g = nc.semaphore(s.name)
        s.h = g.__enter__()
        semguards.append(g)
    final_out = dO.n
    steps["sync"].append(lambda e: e.wait_ge(dO.h, final_out))
    blk = nc.Block()
    block = blk.__enter__()
    for eng_name, meth in (("sync", block.sync), ("scalar", block.scalar),
                           ("vector", block.vector), ("tensor", block.tensor),
                           ("gpsimd", block.gpsimd)):
        def body(e, lst=steps[eng_name]):
            for f in lst:
                f(e)
        meth(body)
    blk.__exit__(None, None, None)
    for g in semguards[::-1]:
        g.__exit__(None, None, None)
    return nc


_NC_CACHE = None


def kernel(L0_attn_x, L0_attn_W1_q, L0_attn_W2_q, L0_attn_W1_k, L0_attn_W2_k,
           L0_attn_W_v, L0_attn_b_v, L0_attn_W_out, L0_attn_b_out,
           L0_ffn_U1, L0_ffn_V1, L0_ffn_b1, L0_ffn_U2, L0_ffn_V2, L0_ffn_b2):
    global _NC_CACHE
    bf = ml_dtypes.bfloat16
    x = np.ascontiguousarray(np.asarray(L0_attn_x, dtype=np.float32))
    B = x.shape[0]

    perm = np.array([(j % 16) * 4 + j // 16 for j in range(64)])
    wq = np.ascontiguousarray(
        np.concatenate([np.asarray(L0_attn_W1_q)[perm],
                        np.asarray(L0_attn_W2_q)[perm]], axis=0).T
        .astype(np.float32))
    wk = np.ascontiguousarray(
        np.concatenate([np.asarray(L0_attn_W1_k)[perm],
                        np.asarray(L0_attn_W2_k)[perm]], axis=0).T
        .astype(np.float32))
    wv = np.ascontiguousarray(np.asarray(L0_attn_W_v, dtype=np.float32).T)
    wout = np.ascontiguousarray(np.asarray(L0_attn_W_out, dtype=np.float32).T)
    bv = np.ascontiguousarray(np.asarray(L0_attn_b_v, dtype=np.float32)[None, :])
    bout = np.ascontiguousarray(
        np.asarray(L0_attn_b_out, dtype=np.float32).reshape(8, 128).T)
    u1 = np.ascontiguousarray(np.asarray(L0_ffn_U1, dtype=np.float32))
    v1 = np.ascontiguousarray(np.asarray(L0_ffn_V1, dtype=np.float32))
    b1 = np.ascontiguousarray(
        np.asarray(L0_ffn_b1, dtype=np.float32).reshape(32, 128).T)
    u2 = np.ascontiguousarray(np.asarray(L0_ffn_U2, dtype=np.float32))
    v2 = np.ascontiguousarray(np.asarray(L0_ffn_V2, dtype=np.float32))
    b2 = np.ascontiguousarray(
        np.asarray(L0_ffn_b2, dtype=np.float32).reshape(8, 128).T)
    ident = np.eye(128, dtype=bf)

    L = np.triu(np.ones((128, 128), dtype=np.float32))
    onesm = np.ones((128, QT), dtype=np.float32)
    zerosm = np.zeros((128, QT), dtype=np.float32)
    tri_a = np.concatenate([L, np.ones((128, 128), dtype=np.float32)], axis=1)
    tri_b = np.concatenate([np.zeros((128, 128), dtype=np.float32), L], axis=1)

    def core_masks(par):
        mk = np.zeros((NSLOT, 4, 128, QT), dtype=np.float32)
        for si in range(NSLOT):
            n = NPROF[si]
            t = TILES[par][si]
            ct = 2 * t + 2
            for j in range(4):
                c = n - 4 + j
                if c < ct - 2:
                    mk[si, j] = onesm
                elif c == ct - 2:
                    mk[si, j] = tri_a
                elif c == ct - 1:
                    mk[si, j] = tri_b
                else:
                    mk[si, j] = zerosm
        return mk.astype(bf)

    masks = {p: core_masks(p) for p in (0, 1)}

    if _NC_CACHE is None:
        _NC_CACHE = build_nc()
    nc = _NC_CACHE

    in_maps = []
    for core in range(8):
        s, par = core // 2, core % 2
        xs = np.ascontiguousarray(x[s])
        xq = np.ascontiguousarray(
            np.concatenate([xs[QT * t:QT * (t + 1)] for t in TILES[par]],
                           axis=0))
        in_maps.append({
            "x": xs, "xq": xq, "masks": masks[par], "ident": ident,
            "wq": wq, "wk": wk, "wv": wv, "bv": bv, "wout": wout,
            "bout": bout, "u1": u1, "v1": v1, "b1": b1, "u2": u2, "v2": v2,
            "b2": b2,
        })

    res = run_bass_kernel_spmd(nc, in_maps, core_ids=list(range(8)))
    y = np.empty((B, T, D), dtype=np.float32)
    for core in range(8):
        s, par = core // 2, core % 2
        yc = res.results[core]["y"]
        for si, t in enumerate(TILES[par]):
            y[s, QT * t:QT * (t + 1)] = yc[QT * si:QT * (si + 1)]
    return y
